# version 52
# baseline (speedup 1.0000x reference)
import math
import numpy as np
import ml_dtypes
import concourse.bass as bass
import concourse.mybir as mybir
from concourse.bass_utils import run_bass_kernel_spmd

F32 = mybir.dt.float32
BF16 = mybir.dt.bfloat16
F8 = mybir.dt.float8e4
AF = mybir.ActivationFunctionType
ALU = mybir.AluOpType
AX = mybir.AxisListType
BF = ml_dtypes.bfloat16

S, E, P, H, D, FF = 2048, 1024, 1024, 8, 128, 4096
EPS_LN = 1e-5
EPS_Q = 1e-5
ISQRT8 = float(1.0 / math.sqrt(8.0))

TAPS = ()
PHASES = 99


def host_constants():
    iden = np.eye(128, dtype=BF)
    Prot = np.zeros((128, 128), np.float32)
    for i in range(64):
        Prot[2 * i, 2 * i + 1] = -1.0
        Prot[2 * i + 1, 2 * i] = 1.0
    invf2 = np.repeat(1.0 / 10000.0 ** (np.arange(0, D, 2) / D), 2)
    u = np.arange(256, dtype=np.float64)
    cosT = np.empty((128, 2048), np.float64)
    sinT = np.empty((128, 2048), np.float64)
    for c in range(8):
        ang = (u * 8 + c)[None, :] * invf2[:, None]
        cosT[:, c * 256:(c + 1) * 256] = np.cos(ang)
        sinT[:, c * 256:(c + 1) * 256] = np.sin(ang)
    return {"c_iden": iden,
            "c_ones_row_bf": np.ones((1, 128), dtype=BF),
            "c_ones_row_f32": np.ones((1, 128), dtype=np.float32),
            "c_ones_col_bf": np.ones((128, 1), dtype=BF),
            "c_ones_col_f32": np.ones((128, 1), dtype=np.float32),
            "c_pt": np.ascontiguousarray(Prot.T).astype(BF),
            "c_iden_f32": np.eye(128, dtype=np.float32),
            "c_cos": cosT.astype(BF), "c_sin": sinT.astype(BF)}


class Prog:
    ENGINES = ("sync", "scalar", "vector", "tensor", "gpsimd")
    SEMS = ("dS", "dG", "dA", "tT", "tV", "tS", "tG", "dV",
            "dB", "dC", "dX", "dY", "dZ")

    def __init__(self, nc):
        self.nc = nc
        self.plan = {e: [] for e in self.ENGINES}
        self.sem = {}
        self.cnt = {}
        self.nops = 0
        for s in self.SEMS:
            self.sem[s] = nc.semaphore(s).__enter__()
            self.cnt[s] = 0

    def op(self, eng, meth, *args, wait=(), inc=None, **kw):
        self.plan[eng].append((meth, args, kw, tuple(wait), inc))
        self.nops += 1
        if inc is not None:
            self.cnt[inc[0]] += inc[1]
            return self.cnt[inc[0]]
        return None

    def at(self, sem):
        return self.cnt[sem]

    def barrier(self, engs=None):
        ws = tuple((s, self.cnt[s]) for s in self.SEMS if self.cnt[s] > 0)
        for e in (engs or self.ENGINES):
            self.plan[e].append((None, (), {}, ws, None))

    def emit(self):
        blk = self.nc.Block().__enter__()
        for eng in self.ENGINES:
            items = self.plan[eng]
            if not items:
                continue

            def body(e, items=items):
                for meth, args, kw, waits, inc in items:
                    for sname, v in waits:
                        e.wait_ge(self.sem[sname], v)
                    if meth is None:
                        continue
                    ins = getattr(e, meth)(*args, **kw)
                    if inc is not None:
                        ins.then_inc(self.sem[inc[0]], inc[1])
            getattr(blk, eng)(body)
        blk.__exit__(None, None, None)


def build_nc(taps=(), phases=99):
    nc = bass.Bass("TRN2", target_bir_lowering=False, debug=False)
    pr = Prog(nc)
    op, at, barrier = pr.op, pr.at, pr.barrier
    C = host_constants()

    d_emb = nc.dram_tensor("emb", [S, E], F32, kind="ExternalInput").ap()
    dw = {n: nc.dram_tensor(n, shp, F32, kind="ExternalInput").ap()
          for n, shp in (("w1", [P, E]), ("w2", [P, E]), ("w3", [P, E]),
                         ("wo", [E, P]), ("wf1", [FF, E]), ("b1", [FF]),
                         ("wf2", [E, FF]), ("b2", [E]))}
    d_out = nc.dram_tensor("out", [S, E], F32, kind="ExternalOutput").ap()
    dc = {k: nc.dram_tensor(k, list(v.shape),
                            BF16 if v.dtype == BF else F32,
                            kind="ExternalInput").ap() for k, v in C.items()}
    d_x1 = nc.dram_tensor("x1s", [S, E], F32, kind="Internal").ap()
    d_q1t = nc.dram_tensor("q1ts", [32, 8, 128, 128], F8, kind="Internal").ap()

    TS = {"wt1": ([128, 8192], BF16), "wtop": ([128, 8192], BF16),
          "at": ([128, 16384], BF16), "qband": ([128, 4096], BF16),
          "kband": ([128, 4096], BF16), "vband": ([128, 4096], BF16),
          "eb": ([128, 8192], BF16), "attn": ([128, 16384], BF16),
          "x1": ([S, E], F32), "a2t": ([128, 4096], BF16),
          "ht": ([128, 16384], BF16), "q2t": ([128, 32768], BF16),
          "q1tt": ([32, 8, 128, 128], BF16), "mhat": ([128, 32], F32),
          "sig": ([128, 2048], F32), "otn": ([128, 2048], BF16),
          "tau": ([128, 8], F32), "xcdbg": ([128, 2048], BF16),
          "dmdbg": ([128, 256], BF16), "scdbg": ([128, 8], F32)}
    tap = {t: nc.dram_tensor("tap_" + t, list(TS[t][0]), TS[t][1],
                             kind="ExternalOutput").ap() for t in taps}

    A = {}

    def sb(name, shape, dtype):
        apx = nc.alloc_sbuf_tensor(name, shape, dtype).ap()
        A[name] = apx
        return apx

    big1 = sb("big1", [128, 32768], BF16)
    s_at = big1[:, 0:16384]
    s_attn = big1[:, 16384:32768]
    s_q2t = big1.bitcast(F8)[:, 0:32768]
    s_q1rez = big1.bitcast(F8)[:, 32768:65536]

    big2 = sb("big2", [128, 24576], BF16)
    s_wt1 = big2[:, 0:8192]
    s_wt2 = big2[:, 8192:16384]
    s_wt3 = big2[:, 16384:24576]
    s_ht = big2[:, 0:16384]
    s_x1q = big2[:, 16384:24576].bitcast(F32)

    big3 = sb("big3", [128, 8192], BF16)
    s_wtop = big3
    s_q1d = (big3[:, 0:1024], big3[:, 1024:2048])
    s_a2t5 = big3[:, 2048:6144]
    s_o5b = big3[:, 6144:8192].bitcast(F32)

    big4 = sb("big4", [128, 8192], BF16)
    s_wnat = big4
    s_et = big4
    s_f5 = big4[:, 0:2048].bitcast(F32)
    s_f5b = big4[:, 2048:4096].bitcast(F32)
    s_b2b = big4[:, 4096:6144].bitcast(F32)
    s_o5 = big4[:, 6144:8192].bitcast(F32)

    big5 = sb("big5", [128, 4096], BF16)
    s_xt = big5.bitcast(F32)
    s_sq = big5
    s_y3 = big5[:, 0:2048].bitcast(F32)
    s_n3 = big5[:, 2048:4096].bitcast(F32)
    s_wfs = big5.bitcast(F32)

    big6 = sb("big6", [128, 4096], BF16)
    s_xc = big6[:, 0:2048]
    s_abq = big6.bitcast(F32)[:, 0:1024]

    s_xc5 = big6[:, 0:2048]

    big7 = sb("big7", [128, 12288], BF16)
    s_qb = big7[:, 0:4096]
    s_kb = big7[:, 4096:8192]
    s_vb = big7[:, 8192:12288]
    s_q2s = big7[:, 0:8192]
    s_q2s2 = (big7[:, 0:8192], big2[:, 0:8192])
    s_wfs4b = big7.bitcast(F32)[:, 4096:6144]
    s_q1stage = big7[:, 8192:10240]

    big8 = sb("big8", [128, 6144], BF16)
    s_sg = big8[:, 0:2048]
    s_otn = big8[:, 2048:4096]
    s_rsigb = big8[:, 4096:5120].bitcast(F32)
    s_rsig = big8[0:1, 5120:6144].bitcast(F32)
    s_x1t3 = big8[:, 4096:6144].bitcast(F32)
    s_q1d4 = (big5.bitcast(F8)[:, 0:1024], big5.bitcast(F8)[:, 1024:2048],
              big5.bitcast(F8)[:, 2048:3072],
              big5.bitcast(F8)[:, 3072:4096])
    s_x1q2 = (s_x1q, big7[:, 2048:10240].bitcast(F32))
    s_a2t5x = (s_a2t5, big8[:, 0:4096])
    s_emb3d = (big8.bitcast(F32)[:, 0:1024],
               big8.bitcast(F32)[:, 1024:2048])
    s_x1t3d = (s_x1t3, big6.bitcast(F32)[:, 1024:2048])

    s_iden = sb("s_iden", [128, 128], BF16)
    s_iden_f32 = sb("s_iden_f32", [128, 128], F32)
    s_b1s = sb("s_b1s", [32, 128], F32)
    s_pt = sb("s_pt", [128, 128], BF16)
    s_ones_row_bf = sb("s_ones_row_bf", [1, 128], BF16)
    s_ones_row_f32 = sb("s_ones_row_f32", [1, 128], F32)
    s_ones_col_bf = sb("s_ones_col_bf", [128, 1], BF16)
    s_ones_col_f32 = sb("s_ones_col_f32", [128, 1], F32)
    s_cos = sb("s_cos", [128, 2048], BF16)
    s_sin = sb("s_sin", [128, 2048], BF16)
    s_stats = sb("s_stats", [128, 40], F32)
    s_bn4 = sb("s_bn4", [128, 8], F32)
    s_mean4 = sb("s_mean4", [128, 4], F32)
    s_rstd4 = sb("s_rstd4", [128, 4], F32)
    s_dmat = sb("s_dmat", [128, 256], BF16)
    s_nm = sb("s_nm", [128, 32], F32)
    s_eb = sb("s_eb", [128, 1], F32)
    s_b1t = sb("s_b1t", [128, 32], F32)
    s_b2row = s_b2b[0:1, :]
    s_gs = sb("s_gs", [128, 32], F32)
    s_tau = sb("s_tau", [128, 4], F32)

    per_part = sum(a.tensor.shape[1] * (2 if a.dtype == BF16 else 4)
                   for a in A.values())
    assert per_part <= 212 * 1024, per_part

    ps_a = nc.alloc_psum_tensor("ps_a", [128, 512], F32).ap()
    ps_b = nc.alloc_psum_tensor("ps_b", [128, 512], F32).ap()
    ps_c = nc.alloc_psum_tensor("ps_c", [128, 1024], F32).ap()
    ps_d = nc.alloc_psum_tensor("ps_d", [128, 1024], F32).ap()
    ps_g = nc.alloc_psum_tensor("ps_g", [128, 1024], F32).ap()
    ps_e = ps_a
    ps_f = ps_b
    bg = {}

    def bank_wait(b, extra=()):
        w = list(extra)
        if id(b.tensor) in bg:
            s, v = bg[id(b.tensor)]
            w.append((s, v))
        return w

    def bank_set(b, sem, v):
        bg[id(b.tensor)] = (sem, v)

    for k, dst in (("c_iden", s_iden), ("c_pt", s_pt),
                   ("c_ones_row_bf", s_ones_row_bf),
                   ("c_ones_row_f32", s_ones_row_f32),
                   ("c_ones_col_bf", s_ones_col_bf),
                   ("c_ones_col_f32", s_ones_col_f32),
                   ("c_iden_f32", s_iden_f32),
                   ("c_cos", s_cos), ("c_sin", s_sin)):
        op("sync", "dma_start", out=dst, in_=dc[k], inc=("dS", 16))
    op("sync", "dma_start", out=s_b1s,
       in_=dw["b1"].rearrange("(t p) -> t p", p=128), inc=("dS", 16))
    t_consts = at("dS")
    tmm = op("tensor", "matmul", ps_f[:, 0:32], s_b1s, s_iden_f32[0:32, 0:32],
             start=True, stop=True, wait=[("dS", t_consts)], inc=("tT", 1))
    tb1 = op("scalar", "copy", s_b1t, ps_f[:, 0:32], wait=[("tT", tmm)],
             inc=("tS", 1))
    bank_set(ps_f, "tS", tb1)

    def transpose_groups(groups, wait_first):
        first = True
        for gi, (lhss, dst) in enumerate(groups):
            bank = ps_a if gi % 2 == 0 else ps_b
            w = bank_wait(bank, wait_first if first else ())
            first = False
            n = len(lhss)
            for q, lhs in enumerate(lhss):
                op("tensor", "matmul", bank[:, q * 128:(q + 1) * 128], lhs,
                   s_iden, start=True, stop=True, wait=w if q == 0 else (),
                   inc=("tT", 1) if q == n - 1 else None)
                w = []
            tmm = at("tT")
            tv = op("scalar", "copy", dst, bank[:, 0:128 * n],
                    wait=[("tT", tmm)], inc=("tS", 1))
            bank_set(bank, "tS", tv)

    s_wnat2 = big1[:, 16384:24576]

    def wnat_load(d_w, buf, sem, wv):
        src = d_w.rearrange("(t p) e -> p t e", p=128)
        return op("gpsimd", "dma_start",
                  out=buf.rearrange("p (t e) -> p t e", t=8), in_=src,
                  inc=(sem, 16), wait=wv)

    def wnat_transposes(buf, s_dst, wait_first, permute=False):
        groups = []
        for ec in range(8):
            for half in range(2):
                if permute:
                    lhss = [buf[:, rb * 1024 + ec: rb * 1024 + 1024: 8]
                            for rb in range(half * 4, half * 4 + 4)]
                else:
                    lhss = [buf[:, pb * 1024 + ec * 128:
                                pb * 1024 + (ec + 1) * 128]
                            for pb in range(half * 4, half * 4 + 4)]
                groups.append((lhss, s_dst[:, ec * 1024 + half * 512:
                                           ec * 1024 + (half + 1) * 512]))
        transpose_groups(groups, wait_first)
        return at("tT")

    if phases >= 0:
        tl1 = wnat_load(dw["w1"], s_wnat, "dG", [("dS", t_consts)])
        tl2 = wnat_load(dw["w2"], s_wnat2, "dZ", [])
        tf1 = wnat_transposes(s_wnat, s_wt1, [("dG", tl1)])
        tl3 = wnat_load(dw["w3"], s_wnat, "dG", [("tT", tf1)])
        tf2 = wnat_transposes(s_wnat2, s_wt2, [("dZ", tl2)])
        tl4 = wnat_load(dw["wo"], s_wnat2, "dZ", [("tT", tf2)])
        wnat_transposes(s_wnat, s_wt3, [("dG", tl3)])
        wnat_transposes(s_wnat2, s_wtop, [("dZ", tl4)], permute=True)
        if "wt1" in tap:
            op("gpsimd", "dma_start", out=tap["wt1"], in_=s_wt1,
               wait=[("tS", at("tS"))], inc=("dG", 16))
        if "wtop" in tap:
            op("gpsimd", "dma_start", out=tap["wtop"], in_=s_wtop,
               wait=[("tS", at("tS"))], inc=("dG", 16))

    def ln_stats_pair(x0, x1, prev_cast_tS):
        w = [("tS", prev_cast_tS)] if prev_cast_tS else []
        op("vector", "bn_stats", s_stats[:, 0:6], x0[:, 0:512], wait=w)
        op("vector", "bn_stats", s_stats[:, 6:12], x0[:, 512:1024])
        op("vector", "bn_stats", s_stats[:, 12:18], x1[:, 0:512])
        tg = op("vector", "bn_stats", s_stats[:, 18:24], x1[:, 512:1024],
                inc=("tV", 1))
        op("vector", "bn_aggr", s_bn4[:, 0:2], s_stats[:, 0:12],
           wait=[("tV", tg)])
        tg = op("vector", "bn_aggr", s_bn4[:, 2:4], s_stats[:, 12:24],
                inc=("tV", 1))
        op("vector", "tensor_scalar", s_mean4[:, 0:2], s_bn4[:, 0:4:2],
           -1.0, None, ALU.mult, wait=[("tV", tg)])
        tg = op("vector", "tensor_scalar", s_rstd4[:, 0:2], s_bn4[:, 1:4:2],
                EPS_LN, None, ALU.add, inc=("tV", 1))
        ts1 = op("scalar", "activation", s_rstd4[:, 0:2], s_rstd4[:, 0:2],
                 AF.Ln, wait=[("tV", tg)], inc=("tS", 1))
        ts2 = op("scalar", "activation", s_rstd4[:, 0:2], s_rstd4[:, 0:2],
                 AF.Exp, scale=-0.5, wait=[("tS", ts1)], inc=("tS", 1))
        return ts2

    def ln_transpose_pair(x0, x1, xc, dst_cols, t_rstd, dst, dst_stride,
                          prev_dmat_tT):
        tcast = 0
        for i, x in enumerate((x0, x1)):
            tcast = op("scalar", "activation", xc[:, i * 1024:(i + 1) * 1024],
                       x, AF.Identity, bias=s_mean4[:, i:i + 1],
                       wait=[("tS", t_rstd)] if i == 0 else (),
                       inc=("tS", 1))
        w = [("tS", t_rstd)]
        if prev_dmat_tT:
            w.append(("tT", prev_dmat_tT))
        td = 0
        for i in range(2):
            td = op("vector", "tensor_scalar",
                    s_dmat[:, i * 128:(i + 1) * 128],
                    s_iden, s_rstd4[:, i:i + 1], None, ALU.mult,
                    wait=w if i == 0 else (), inc=("tV", 1))
            w = []
        last = 0
        for ec in range(8):
            bank = ps_a if ec % 2 == 0 else ps_b
            w = bank_wait(bank, [("tS", tcast), ("tV", td)])
            for i in range(2):
                op("tensor", "matmul", bank[:, i * 128:(i + 1) * 128],
                   xc[:, i * 1024 + ec * 128: i * 1024 + (ec + 1) * 128],
                   s_dmat[:, i * 128:(i + 1) * 128], start=(i == 0),
                   stop=(i == 1),
                   wait=w if i == 0 else (),
                   inc=("tT", 1) if i == 1 else None)
                w = []
            tmm = at("tT")
            tv = op("scalar", "copy",
                    dst[:, ec * dst_stride + dst_cols: ec * dst_stride + dst_cols + 256],
                    bank[:, 0:256], wait=[("tT", tmm)], inc=("tS", 1))
            bank_set(bank, "tS", tv)
            last = tv
        return last, at("tT")

    if phases >= 1:
        xt4 = (big5.bitcast(F32)[:, 0:1024], big5.bitcast(F32)[:, 1024:2048],
               big8.bitcast(F32)[:, 0:1024], big8.bitcast(F32)[:, 1024:2048])
        xsems = ("dS", "dA", "dB", "dC")
        prev_cast2 = [0, 0]
        prev_dm = 0
        prev_cast = 0
        for g in range(8):
            p = g % 2
            tl = []
            for i in range(2):
                t = g * 2 + i
                n4 = p * 2 + i
                wld = ([("tS", prev_cast2[p])] if prev_cast2[p]
                       else [("dS", t_consts)])
                tl.append(op(("sync", "scalar")[i], "dma_start", out=xt4[n4],
                             in_=d_emb[t * 128:(t + 1) * 128, :],
                             wait=wld, inc=(xsems[n4], 16)))
            x0 = xt4[p * 2]
            x1 = xt4[p * 2 + 1]
            op("vector", None, wait=[(xsems[p * 2], tl[0]),
                                     (xsems[p * 2 + 1], tl[1])])
            t_rstd = ln_stats_pair(x0, x1, 0)
            prev_cast, prev_dm = ln_transpose_pair(
                x0, x1, s_xc, g * 256, t_rstd, s_at, 2048, prev_dm)
            prev_cast2[p] = prev_cast
            if g == 1 and "xcdbg" in tap:
                op("gpsimd", "dma_start", out=tap["xcdbg"], in_=s_xc,
                   wait=[("tS", prev_cast)], inc=("dG", 16))
                op("gpsimd", "dma_start", out=tap["dmdbg"], in_=s_dmat,
                   wait=[("tS", prev_cast)], inc=("dG", 16))
                op("gpsimd", "dma_start", out=tap["scdbg"][:, 0:2],
                   in_=s_mean4, wait=[("tS", prev_cast)], inc=("dG", 16))
                op("gpsimd", "dma_start", out=tap["scdbg"][:, 2:4],
                   in_=s_rstd4, wait=[("tS", prev_cast)], inc=("dG", 16))
                op("gpsimd", "dma_start", out=tap["scdbg"][:, 4:8],
                   in_=s_bn4, wait=[("tS", prev_cast)], inc=("dG", 16))
                barrier()
        t_p1_done = at("tS")
        if "at" in tap:
            op("gpsimd", "dma_start", out=tap["at"], in_=s_at,
               wait=[("tS", t_p1_done)], inc=("dG", 16))

    p4a_wfsA = big5.bitcast(F32)[:, 0:1024]
    p4a_wfsB = big5.bitcast(F32)[:, 1024:2048]
    p4a_stage = big5.bitcast(F8)[:, 4096:6144]
    p4a_A = big6[:, 0:1024]
    p4a_B = big6[:, 1024:2048]
    p4a_pair = big6[:, 2048:4096]

    def p4a_gen(first_wait):
        src_w1 = dw["wf1"].rearrange("(t p) e -> t p e", p=128)
        red = [0] * 32
        for t in range(32):
            buf = p4a_wfsA if t % 2 == 0 else p4a_wfsB
            qeng, qsem = ("sync", "dS") if t % 2 == 0 else ("scalar", "dA")
            w = [("tV", red[t - 2])] if t >= 2 else list(first_wait)
            tl = op(qeng, "dma_start", out=buf, in_=src_w1[t], inc=(qsem, 16),
                    wait=w)
            red[t] = op("vector", "reduce_sum", s_stats[:, t:t + 1], buf,
                        axis=AX.X, apply_absolute_value=True,
                        wait=[(qsem, tl)], inc=("tV", 1))
            yield
        tg = op("vector", "reduce_sum", s_stats[:, 32:33], s_stats[:, 0:32],
                axis=AX.X, wait=[("tV", red[31])], inc=("tV", 1))
        w = bank_wait(ps_f, [("tV", tg)])
        tmm = op("tensor", "matmul", ps_f[0:1, 0:1], s_ones_col_f32,
                 s_stats[:, 32:33], start=True, stop=True, wait=w,
                 inc=("tT", 1))
        tg = op("vector", "tensor_scalar", s_stats[0:1, 33:34],
                ps_f[0:1, 0:1], 0.5 / (FF * E), 0.5 * EPS_Q, ALU.mult,
                op1=ALU.add, wait=[("tT", tmm)], inc=("tV", 1))
        bank_set(ps_f, "tV", tg)
        w = bank_wait(ps_f, [("tV", tg)])
        tmm = op("tensor", "matmul", ps_f[:, 1:2], s_ones_row_f32,
                 s_stats[0:1, 33:34], start=True, stop=True, wait=w,
                 inc=("tT", 1))
        ts = op("scalar", "copy", s_tau[:, 0:1], ps_f[:, 1:2],
                wait=[("tT", tmm)], inc=("tS", 1))
        bank_set(ps_f, "tS", ts)
        t_tau = op("vector", "tensor_scalar", s_tau[:, 1:2], s_tau[:, 0:1],
                   -1.0, None, ALU.mult, wait=[("tS", ts)], inc=("tV", 1))
        yield
        prev_q = 0
        prev_tp = 0
        prev_st = 0
        stage4 = p4a_stage.rearrange("p (ec q f) -> p ec q f", ec=8, q=2)
        for gp in range(16):
            for i in range(2):
                ft = gp * 2 + i
                qeng, qsem = ("sync", "dS") if i == 0 else ("scalar", "dA")
                w = [("tV", prev_q if gp or i else t_tau)]
                tl = op(qeng, "dma_start", out=p4a_wfsA,
                        in_=src_w1[ft], inc=(qsem, 16), wait=w)
                wv = [(qsem, tl), ("tT", prev_tp)]
                prev_q = quantize_p4a(p4a_pair[:, i * 1024:(i + 1) * 1024],
                                      p4a_wfsA, wv)
                if i == 0:
                    yield
            for ec in range(8):
                bankp = ps_f if ec % 2 == 0 else ps_e
                w = bank_wait(bankp, [("tV", prev_q)] if ec == 0 else ())
                if ec == 0 and prev_st:
                    w.append(("dG", prev_st))
                for i in range(2):
                    op("tensor", "matmul",
                       bankp[:, i * 128:(i + 1) * 128],
                       p4a_pair[:, i * 1024 + ec * 128:
                                i * 1024 + (ec + 1) * 128],
                       s_iden, start=(i == 0), stop=(i == 1),
                       wait=w if i == 0 else (),
                       inc=("tT", 1) if i == 1 else None)
                    w = []
                tv = op("scalar", "copy", p4a_stage[:, ec * 256:(ec + 1) * 256],
                        bankp[:, 0:256],
                        wait=[("tT", at("tT"))], inc=("tS", 1))
                bank_set(bankp, "tS", tv)
                if ec % 2 == 1:
                    yield
            prev_tp = at("tT")
            tev = at("tS")
            for i in range(2):
                ft = gp * 2 + i
                prev_st = op("gpsimd", "dma_start",
                             out=d_q1t[ft].rearrange("ec e f -> e ec f"),
                             in_=stage4[:, :, i, :], inc=("dG", 16),
                             wait=[("tS", tev)] if i == 0 else ())
            yield
        while True:
            yield

    def quantize_p4a(dst, srcb, wv):
        wv = list(wv)
        if q_chain[0]:
            wv.append(("tV", q_chain[0]))
        tq = op("vector", "tensor_single_scalar", p4a_B, srcb,
                s_tau[:, 1:2], ALU.is_lt, wait=wv, inc=("tV", 1))
        q_chain[0] = op("vector", "scalar_tensor_tensor", dst, srcb,
                        s_tau[:, 0:1], p4a_B, ALU.is_gt, ALU.subtract,
                        wait=[("tV", tq)], inc=("tV", 1))
        return q_chain[0]

    q_chain = [0]
    p4a_iter = [None]

    def p4a_step():
        if p4a_iter[0] is not None:
            next(p4a_iter[0])

    if phases >= 2:
        t_att = 0
        ebuf_guard = [0, 0]
        if phases >= 4:
            p4a_iter[0] = p4a_gen([("tS", t_p1_done)])
        pairs3 = (ps_c, ps_d, ps_g)
        s_e0 = big8[:, 0:2048]
        s_e1 = big8[:, 2048:4096]
        s_otn2 = big8[:, 4096:4608]
        s_otn2b = big8[:, 4608:5120]
        etv = s_et.rearrange("p (kc q) -> p kc q", kc=16)
        ebuf_of = (s_e0, s_e1)
        exp_done = [0, 0, 0]
        nm_guard = [0, 0, 0]
        ebuf_guard = [0, 0]
        tpk = [0]
        ST = {}
        pend_group = [None]
        pend_ot = [None]
        grp_rs = []

        def u_scores(band, hh, qb, qg, kh):
            u = 2 * qb + kh
            bank = pairs3[u % 3]
            cq, qhalf = qg // 2, qg % 2
            lhs_q = s_qb[:, hh * 2048 + cq * 256 + qhalf * 128:
                         hh * 2048 + cq * 256 + (qhalf + 1) * 128]
            w = bank_wait(bank)
            if qb == 0 and kh == 0:
                w.append(("tV", rope_v_of[band]))
            for half in range(2):
                rhs = s_kb[:, hh * 2048 + kh * 1024 + half * 512:
                           hh * 2048 + kh * 1024 + (half + 1) * 512]
                op("tensor", "matmul", bank[:, half * 512:(half + 1) * 512],
                   lhs_q, rhs, start=True, stop=True,
                   wait=w if half == 0 else (),
                   inc=("tT", 1) if half == 1 else None)
                w = []
            return at("tT")

        def u_maxes(qb, kh, tmm):
            u = 2 * qb + kh
            c0 = (qb % 3) * 6
            bank = pairs3[u % 3]
            wmx = [("tT", tmm)]
            if kh == 0 and nm_guard[qb % 3]:
                wmx.append(("tS", nm_guard[qb % 3]))
            tmax = op("vector", "reduce_max", s_nm[:, c0 + kh:c0 + kh + 1],
                      bank, axis=AX.X, wait=wmx, inc=("tV", 1))
            if kh == 0:
                return ("tS", op("scalar", "activation",
                                 s_nm[:, c0 + 3:c0 + 4], s_nm[:, c0:c0 + 1],
                                 AF.Identity, scale=-ISQRT8,
                                 wait=[("tV", tmax)], inc=("tS", 1)))
            tm = op("vector", "reduce_max", s_nm[:, c0 + 2:c0 + 3],
                    s_nm[:, c0:c0 + 2], axis=AX.X, wait=[("tV", tmax)],
                    inc=("tV", 1))
            tnb = op("scalar", "activation", s_nm[:, c0 + 4:c0 + 5],
                     s_nm[:, c0 + 2:c0 + 3], AF.Identity, scale=-ISQRT8,
                     wait=[("tV", tm)], inc=("tS", 1))
            td = op("scalar", "activation", s_nm[:, c0 + 5:c0 + 6],
                    s_nm[:, c0:c0 + 1], AF.Identity, scale=ISQRT8,
                    bias=s_nm[:, c0 + 4:c0 + 5],
                    wait=[("tS", tnb)], inc=("tS", 1))
            return ("tS", td)

        def u_exps(qb, kh, tnb):
            u = 2 * qb + kh
            bank = pairs3[u % 3]
            c0 = (qb % 3) * 6
            a0 = (qb % 2) * 2
            ebuf = ebuf_of[qb % 2]
            wx = [tnb]
            if kh == 0:
                wx.append(("tT", ebuf_guard[qb % 2]))
            tse = op("scalar", "activation", ebuf[:, kh * 1024:(kh + 1) * 1024],
                     bank, AF.Exp,
                     bias=s_nm[:, c0 + 3 + kh:c0 + 4 + kh], scale=ISQRT8,
                     accum_out=s_gs[:, a0 + kh:a0 + kh + 1],
                     wait=wx, inc=("tS", 1))
            bank_set(bank, "tS", tse)
            exp_done[u % 3] = tse
            if kh == 1:
                st = ST[qb]
                st["tf0"] = op("scalar", "activation",
                               s_gs[:, 24 + qb % 2:25 + qb % 2],
                               s_nm[:, (qb % 3) * 6 + 5:(qb % 3) * 6 + 6],
                               AF.Exp, wait=[tnb], inc=("tS", 1))
                nm_guard[qb % 3] = st["tf0"]
                st["f0g"] = op("scalar", "copy", s_tau[0:1, 2:3],
                               s_nm[0:1, (qb % 3) * 6 + 4:(qb % 3) * 6 + 5],
                               inc=("tS", 1))
            return tse

        def u_sigma_d0(qb):
            st = ST[qb]
            a0 = (qb % 2) * 2
            tsg = op("vector", "scalar_tensor_tensor",
                     s_gs[:, 12 + qb % 4:13 + qb % 4], s_gs[:, a0:a0 + 1],
                     s_gs[:, 24 + qb % 2:25 + qb % 2],
                     s_gs[:, a0 + 1:a0 + 2], ALU.mult, ALU.add,
                     wait=[("tS", st["f0g"])], inc=("tV", 1))
            st["trs"] = op("vector", "reciprocal",
                           s_gs[:, 16 + qb % 8:17 + qb % 8],
                           s_gs[:, 12 + qb % 4:13 + qb % 4],
                           wait=[("tV", tsg)], inc=("tV", 1))
            st["td0"] = op("scalar", "activation",
                           s_dmat[:, (qb % 2) * 128:(qb % 2) * 128 + 128],
                           s_iden, AF.Identity,
                           scale=s_gs[:, 24 + qb % 2:25 + qb % 2],
                           wait=[("tS", st["tf0"]),
                                 ("tT", ebuf_guard[qb % 2])],
                           inc=("tS", 1))

        def u_transposes(qb, kh):
            st = ST[qb]
            qj = st["qj"]
            ebuf = ebuf_of[qb % 2]
            d0 = s_dmat[:, (qb % 2) * 128:(qb % 2) * 128 + 128]
            for kg in range(2):
                bank = ps_a if tpk[0] % 2 == 0 else ps_b
                tpk[0] += 1
                w = bank_wait(bank)
                if kg == 0 and kh == 0:
                    w += [("tS", st["td0"]), ("tS", exp_done[(2 * qb + 1) % 3])]
                for t4 in range(4):
                    kc = kh * 8 + kg * 4 + t4
                    tl4 = op("tensor", "matmul",
                             bank[:, t4 * 128:(t4 + 1) * 128],
                             ebuf[:, kc * 128:(kc + 1) * 128],
                             d0 if kc < 8 else s_iden,
                             start=(t4 == 0), stop=(t4 == 3),
                             wait=w if t4 == 0 else (),
                             inc=("tT", 1) if t4 == 3 else None)
                    w = []
                kg0 = kh * 2 + kg
                dstv = etv[:, kg0 * 4:kg0 * 4 + 4, qj * 128:(qj + 1) * 128]
                srcv = bank.rearrange("p (a b) -> p a b", a=4)
                if kg0 != 3:
                    tv = op("scalar", "copy", dstv, srcv,
                            wait=[("tT", at("tT"))], inc=("tS", 1))
                    bank_set(bank, "tS", tv)
                else:
                    tv = op("vector", "tensor_copy", dstv, srcv,
                            wait=[("tT", at("tT"))], inc=("tV", 1))
                    bank_set(bank, "tV", tv)
            if kh == 1:
                ebuf_guard[qb % 2] = tl4
                st["t_et"] = [("tS", at("tS")), ("tV", at("tV"))]

        def emit_group_av(gst):
            hh = gst["hh"]
            bankA = ps_a if tpk[0] % 2 == 0 else ps_b
            bankB = ps_b if tpk[0] % 2 == 0 else ps_a
            tpk[0] += 2
            ws = [bank_wait(bankA, list(gst["t_et"])), bank_wait(bankB)]
            tmm2 = [0, 0]
            for kk in range(8):
                for j in range(2):
                    kc = j * 8 + kk
                    bank = bankA if j == 0 else bankB
                    lhs = s_vb[:, (hh * 2 + kc % 2) * 1024 + (kc // 2) * 128:
                               (hh * 2 + kc % 2) * 1024 + (kc // 2 + 1) * 128]
                    tmm2[j] = op("tensor", "matmul", bank, lhs,
                                 s_et[:, kc * 512:(kc + 1) * 512],
                                 start=(kk == 0), stop=(kk == 7),
                                 wait=ws[j] if kk == 0 else (),
                                 inc=("tT", 1) if kk == 7 else None)
                    ws[j] = []
            tA = op("vector", "tensor_copy", s_otn2, bankA,
                    wait=[("tT", tmm2[0])], inc=("tV", 1))
            tB = op("vector", "tensor_copy", s_otn2b, bankB,
                    wait=[("tT", tmm2[1])], inc=("tV", 1))
            bank_set(bankA, "tV", tA)
            bank_set(bankB, "tV", tB)
            gst["t_oev"] = [("tV", tB)]

        def emit_group(gst):
            h, g = gst["h"], gst["g"]
            bank2 = ps_a if tpk[0] % 2 == 0 else ps_b
            tpk[0] += 1
            w = bank_wait(bank2, list(gst["t_oev"]))
            for qt in range(4):
                for srcb in (s_otn2, s_otn2b):
                    op("tensor", "matmul",
                       bank2[:, qt * 128:(qt + 1) * 128],
                       srcb[:, qt * 128:(qt + 1) * 128], s_iden,
                       start=(qt == 0 and srcb is s_otn2),
                       stop=(qt == 3 and srcb is s_otn2b),
                       wait=w if (qt == 0 and srcb is s_otn2) else (),
                       inc=("tT", 1) if (qt == 3 and srcb is s_otn2b)
                       else None)
                    w = []
            qid0 = gst["qb"] - 3
            for qt in range(4):
                tile = g * 4 + qt
                wq = [("tT", at("tT"))] if qt == 0 else []
                wq.append(("tV", gst["trs"][qt]))
                op("vector", "tensor_scalar",
                   s_attn[:, tile * 1024 + h * 128:tile * 1024 + (h + 1) * 128],
                   bank2[:, qt * 128:(qt + 1) * 128],
                   s_gs[:, 16 + (qid0 + qt) % 8:17 + (qid0 + qt) % 8],
                   None, ALU.mult,
                   wait=wq, inc=("tV", 1) if qt == 3 else None)
            bank_set(bank2, "tV", at("tV"))

        def drive_band(band, h0):
            units = []
            for hh in range(2):
                h = h0 + hh
                for g in range(4):
                    for qj in range(4):
                        qg = g * 4 + qj
                        qb = hh * 16 + qg
                        for kh in range(2):
                            units.append((hh, h, g, qj, qb, qg, kh))
            n = len(units)
            for ui in range(n + 5):
                if 0 <= ui - 2 < n:
                    qb2, kh2 = units[ui - 2][4], units[ui - 2][6]
                    u_exps(qb2, kh2, ST[qb2]["tnb%d" % kh2])
                if ui % 2 == 0:
                    if pend_ot[0] is not None:
                        emit_group(pend_ot[0])
                        pend_ot[0] = None
                    elif pend_group[0] is not None:
                        emit_group_av(pend_group[0])
                        pend_ot[0] = pend_group[0]
                        pend_group[0] = None
                if 0 <= ui - 4 < n:
                    qb4, kh4 = units[ui - 4][4], units[ui - 4][6]
                    if kh4 == 0:
                        u_sigma_d0(qb4)
                    u_transposes(qb4, kh4)
                    if kh4 == 1:
                        stp = ST[qb4]
                        grp_rs.append(stp["trs"])
                        if stp["qj"] == 3:
                            assert pend_group[0] is None
                            pend_group[0] = dict(stp, trs=grp_rs[-4:])
                        del ST[qb4]
                if ui < n:
                    hh, h, g, qj, qb, qg, kh = units[ui]
                    ST.setdefault(qb, {"qb": qb, "hh": hh, "h": h, "g": g,
                                       "qj": qj})
                    tmm = u_scores(band, hh, qb, qg, kh)
                    ST[qb]["tnb%d" % kh] = u_maxes(qb, kh, tmm)
                if ui % 2 == 1:
                    p4a_step()
            if pend_group[0] is not None:
                emit_group_av(pend_group[0])
                pend_ot[0] = pend_group[0]
                pend_group[0] = None
            if pend_ot[0] is not None:
                emit_group(pend_ot[0])
                pend_ot[0] = None

        rope_v_of = {}
        for band in range(4):
            barrier()
            h0 = 2 * band
            bcol = band * 512
            ev_qk = {id(s_qb): [0] * 8, id(s_kb): [0] * 8}

            def emit_vt(vt):
                bank = ps_c if vt % 2 == 0 else ps_d
                w = bank_wait(bank)
                for ec in range(8):
                    op("tensor", "matmul", bank[:, 0:512],
                       s_at[:, ec * 2048 + bcol + vt * 128:
                            ec * 2048 + bcol + (vt + 1) * 128],
                       s_wt3[:, ec * 1024: ec * 1024 + 512],
                       start=(ec == 0), stop=(ec == 7),
                       wait=w if ec == 0 else ())
                    w = []
                    op("tensor", "matmul", bank[:, 512:1024],
                       s_at[:, ec * 2048 + bcol + vt * 128:
                            ec * 2048 + bcol + (vt + 1) * 128],
                       s_wt3[:, ec * 1024 + 512: (ec + 1) * 1024],
                       start=(ec == 0), stop=(ec == 7),
                       inc=("tT", 1) if ec == 7 else None)
                tv = op("scalar", "copy", s_vb[:, vt * 1024:(vt + 1) * 1024],
                        bank, wait=[("tT", at("tT"))], inc=("tS", 1))
                bank_set(bank, "tS", tv)

            rope_pend = []

            def emit_rope(dstb, cs):
                d3r = dstb.rearrange("p (h c2 x) -> p h c2 x", h=2, c2=8)
                for c in cs:
                    bankr = (ps_g[:, 0:512] if c % 2 == 0
                             else ps_g[:, 512:1024])
                    w = bank_wait(ps_g, [("tS", ev_qk[id(dstb)][c])])
                    tmm = op("tensor", "matmul", bankr, s_pt, d3r[:, :, c, :],
                             start=True, stop=True, wait=w, inc=("tT", 1))
                    tabc = s_cos[:, c * 256:(c + 1) * 256]
                    tabs = s_sin[:, c * 256:(c + 1) * 256]
                    for hh in range(2):
                        ss = dstb[:, hh * 2048 + c * 256:
                                  hh * 2048 + (c + 1) * 256]
                        rr = bankr[:, hh * 256:(hh + 1) * 256]
                        op("vector", "tensor_tensor", ss, ss, tabc, ALU.mult,
                           wait=[("tT", tmm)] if hh == 0 else (),
                           inc=("tV", 1))
                        tv2 = op("vector", "tensor_tensor", rr, rr, tabs,
                                 ALU.mult, inc=("tV", 1))
                        op("vector", "tensor_tensor", ss, ss, rr, ALU.add,
                           wait=[("tV", tv2)], inc=("tV", 1))
                    bank_set(ps_g, "tV", at("tV"))

            qkp = 0
            for (wt, dstb) in ((s_wt1, s_qb), (s_wt2, s_kb)):
                for cp in range(4):
                    cs = (2 * cp, 2 * cp + 1)
                    ws = [bank_wait(ps_a), bank_wait(ps_b)]
                    tmm2 = [0, 0]
                    for ec in range(8):
                        for j in range(2):
                            c = cs[j]
                            bank = ps_a if j == 0 else ps_b
                            tmm2[j] = op(
                                "tensor", "matmul", bank,
                                wt[:, ec * 1024 + c * 128:
                                   ec * 1024 + (c + 1) * 128],
                                s_at[:, ec * 2048 + bcol:
                                     ec * 2048 + bcol + 512],
                                start=(ec == 0), stop=(ec == 7),
                                wait=ws[j] if ec == 0 else (),
                                inc=("tT", 1) if ec == 7 else None)
                            ws[j] = []
                    for j in range(2):
                        c = cs[j]
                        bank = ps_a if j == 0 else ps_b
                        d3 = dstb.rearrange("p (h c2 x) -> p h c2 x",
                                            h=2, c2=8)
                        tv = op("scalar", "copy", d3[:, :, c, :],
                                bank.rearrange("p (h x) -> p h x", h=2),
                                wait=[("tT", tmm2[j])], inc=("tS", 1))
                        ev_qk[id(dstb)][c] = tv
                        bank_set(bank, "tS", tv)
                    rope_pend.append((dstb, cs))
                    if len(rope_pend) > 1:
                        emit_rope(*rope_pend.pop(0))
                    if qkp % 2 == 1:
                        emit_vt(qkp // 2)
                    qkp += 1
            while rope_pend:
                emit_rope(*rope_pend.pop(0))
            t_b1_s = at("tS")
            if "vband" in tap and band == 0:
                op("gpsimd", "dma_start", out=tap["vband"], in_=s_vb,
                   wait=[("tS", t_b1_s)], inc=("dG", 16))
            rope_v = at("tV")
            if "qband" in tap and band == 0:
                op("gpsimd", "dma_start", out=tap["qband"], in_=s_qb,
                   wait=[("tV", rope_v)], inc=("dG", 16))
                op("gpsimd", "dma_start", out=tap["kband"], in_=s_kb,
                   wait=[("tV", rope_v)], inc=("dG", 16))
            rope_v_of[band] = rope_v
            drive_band(band, h0)
        if "attn" in tap:
            op("gpsimd", "dma_start", out=tap["attn"], in_=s_attn,
               wait=[("tS", at("tS"))], inc=("dG", 16))

    def gamma_pass_gen(d_w, rows, cols, gcol, out):
        npieces = (rows // 128) * (cols // 1024)
        per_row = cols // 1024
        assert npieces <= 32
        src3 = d_w.rearrange("(t p) (q e) -> t q p e", p=128, e=1024)
        red = [0] * npieces
        gbufs = (s_wfs4b[:, 0:1024], s_wfs4b[:, 1024:2048],
                 s_wfs.bitcast(F32)[:, 0:1024], s_wfs.bitcast(F32)[:, 1024:2048])
        gsems = ("dS", "dA", "dB", "dC")
        for t in range(npieces):
            buf = gbufs[t % 4]
            qeng = ("sync", "scalar")[t % 2]
            qsem = gsems[t % 4]
            w = [("tV", red[t - 4])] if t >= 4 else ()
            tl = op(qeng, "dma_start", out=buf,
                    in_=src3[t // per_row, t % per_row], inc=(qsem, 16),
                    wait=w)
            red[t] = op("vector", "reduce_sum", s_gs[:, t:t + 1], buf,
                        axis=AX.X, apply_absolute_value=True,
                        wait=[(qsem, tl)], inc=("tV", 1))
            yield
        tg = op("vector", "reduce_sum", s_nm[:, 22:23], s_gs[:, 0:npieces],
                axis=AX.X, wait=[("tV", red[npieces - 1])], inc=("tV", 1))
        w = bank_wait(ps_f, [("tV", tg)])
        tmm = op("tensor", "matmul", ps_f[0:1, 0:1], s_ones_col_f32,
                 s_nm[:, 22:23], start=True, stop=True, wait=w, inc=("tT", 1))
        tg = op("vector", "tensor_scalar", s_nm[0:1, 20:21], ps_f[0:1, 0:1],
                0.5 / (rows * cols), 0.5 * EPS_Q, ALU.mult, op1=ALU.add,
                wait=[("tT", tmm)], inc=("tV", 1))
        bank_set(ps_f, "tV", tg)
        w = bank_wait(ps_f, [("tV", tg)])
        tmm = op("tensor", "matmul", ps_f[:, 1:2], s_ones_row_f32,
                 s_nm[0:1, 20:21], start=True, stop=True, wait=w,
                 inc=("tT", 1))
        ts = op("scalar", "copy", s_tau[:, gcol:gcol + 1], ps_f[:, 1:2],
                wait=[("tT", tmm)], inc=("tS", 1))
        bank_set(ps_f, "tS", ts)
        out[0] = op("vector", "tensor_scalar", s_tau[:, gcol + 1:gcol + 2],
                    s_tau[:, gcol:gcol + 1], -1.0, None, ALU.mult,
                    wait=[("tS", ts)], inc=("tV", 1))
        yield

    q_chain = [0]

    def quantize_into(dst, src, gcol, wv):
        wv = list(wv)
        if q_chain[0]:
            wv.append(("tV", q_chain[0]))
        tq = op("vector", "tensor_single_scalar", s_abq[:, 0:src.shape[1]],
                src, s_tau[:, gcol + 1:gcol + 2], ALU.is_lt, wait=wv,
                inc=("tV", 1))
        q_chain[0] = op("vector", "scalar_tensor_tensor", dst, src,
                        s_tau[:, gcol:gcol + 1], s_abq[:, 0:src.shape[1]],
                        ALU.is_gt, ALU.subtract, wait=[("tV", tq)],
                        inc=("tV", 1))
        return q_chain[0]

    def p4b_gen():
        tau_out = [0]
        for _ in gamma_pass_gen(dw["wf2"], E, FF, 2, tau_out):
            yield
        t_tau = tau_out[0]
        prev_q = [0, 0, 0, 0]
        qbufs = (s_wfs4b[:, 0:1024], s_wfs4b[:, 1024:2048],
                 s_wfs.bitcast(F32)[:, 0:1024],
                 s_wfs.bitcast(F32)[:, 1024:2048])
        qsems4 = ("dS", "dA", "dB", "dC")
        tp_hist = [0, 0]
        for pair in range(4):
            q2sb = s_q2s2[pair % 2]
            for i in range(2):
                et = pair * 2 + i
                for qq in range(4):
                    n4 = qq
                    buf = qbufs[n4]
                    qeng = ("sync", "scalar")[qq % 2]
                    qsem = qsems4[n4]
                    w = []
                    if pair > 0 or i > 0:
                        w.append(("tV", prev_q[n4]))
                    else:
                        w.append(("tV", t_tau))
                    tl = op(qeng, "dma_start", out=buf,
                            in_=dw["wf2"][et * 128:(et + 1) * 128,
                                          qq * 1024:(qq + 1) * 1024],
                            inc=(qsem, 16), wait=w)
                    wv = [(qsem, tl)]
                    if pair == 0 and i == 0 and qq == 0:
                        wv.append(("tV", t_tau))
                    if i == 0 and qq < 2 and tp_hist[pair % 2]:
                        wv.append(("tT", tp_hist[pair % 2]))
                    prev_q[n4] = quantize_into(
                        q2sb[:, i * 4096 + qq * 1024:
                             i * 4096 + (qq + 1) * 1024], buf, 2, wv)
                    yield
            tq = at("tV")
            for fcg in range(4):
                for fc in range(fcg * 8, fcg * 8 + 8):
                    bank = ps_a if fc % 2 == 0 else ps_b
                    w = bank_wait(bank, [("tV", tq)])
                    for i in range(2):
                        op("tensor", "matmul",
                           bank[:, i * 128:(i + 1) * 128],
                           q2sb[:, i * 4096 + fc * 128:
                                i * 4096 + (fc + 1) * 128],
                           s_iden, start=True, stop=True,
                           wait=w if i == 0 else (),
                           inc=("tT", 1) if i == 1 else None)
                        w = []
                    tv = op("scalar", "copy",
                            s_q2t[:, fc * 1024 + pair * 256:
                                  fc * 1024 + (pair + 1) * 256],
                            bank[:, 0:256], wait=[("tT", at("tT"))],
                            inc=("tS", 1))
                    bank_set(bank, "tS", tv)
                yield
            tp_hist[pair % 2] = at("tT")
        while True:
            yield

    p4b_iter = [None]

    def p4b_step():
        if p4b_iter[0] is not None:
            next(p4b_iter[0])

    if phases >= 3:
        barrier()
        prev_stats_tV = 0
        prev_add2 = [0, 0]
        prev_st2 = [0, 0]
        prev_ts3 = 0
        t_embk = [0] * 16
        for k in range(16):
            u, rb = k // 8, k % 8
            bank = ps_c if k % 2 == 0 else ps_d
            w = bank_wait(bank)
            for c in range(8):
                lhs = s_wtop[:, c * 1024 + rb * 128: c * 1024 + (rb + 1) * 128]
                rhs = s_attn[:, (2 * c + u) * 1024:(2 * c + u + 1) * 1024]
                op("tensor", "matmul", bank[:, 0:512], lhs, rhs[:, 0:512],
                   start=(c == 0), stop=(c == 7), wait=w if c == 0 else ())
                w = []
                op("tensor", "matmul", bank[:, 512:1024], lhs,
                   rhs[:, 512:1024], start=(c == 0), stop=(c == 7),
                   inc=("tT", 1) if c == 7 else None)
            wy = [("tT", at("tT"))]
            if prev_stats_tV:
                wy.append(("tV", prev_stats_tV))
            if prev_ts3:
                wy.append(("tS", prev_ts3))
            tev = op("scalar", "copy", s_y3, bank, wait=wy, inc=("tS", 1))
            bank_set(bank, "tS", tev)
            row = k * 128
            eng3, sem3 = ("sync", "dB") if k % 2 == 0 else ("scalar", "dC")
            we = [("tV", prev_add2[k % 2])] if prev_add2[k % 2] else []
            t_embk[k] = op(eng3, "dma_start", out=s_emb3d[k % 2],
                           in_=d_emb[row:row + 128, :], wait=we,
                           inc=(sem3, 16))
            op("vector", "bn_stats", s_stats[:, 0:6], s_y3[:, 0:512],
               wait=[("tS", tev)])
            prev_stats_tV = op("vector", "bn_stats", s_stats[:, 6:12],
                               s_y3[:, 512:1024], inc=("tV", 1))
            tg = op("vector", "bn_aggr", s_bn4[:, 0:2], s_stats[:, 0:12],
                    wait=[("tV", prev_stats_tV)], inc=("tV", 1))
            tg = op("vector", "tensor_scalar", s_rstd4[:, 0:1], s_bn4[:, 1:2],
                    EPS_LN, None, ALU.add, wait=[("tV", tg)], inc=("tV", 1))
            ts1 = op("scalar", "activation", s_rstd4[:, 0:1], s_rstd4[:, 0:1],
                     AF.Ln, wait=[("tV", tg)], inc=("tS", 1))
            ts2 = op("scalar", "activation", s_rstd4[:, 0:1], s_rstd4[:, 0:1],
                     AF.Exp, scale=-0.5, wait=[("tS", ts1)], inc=("tS", 1))
            tg = op("vector", "tensor_tensor", s_mean4[:, 0:1], s_bn4[:, 0:1],
                    s_rstd4[:, 0:1], ALU.mult, wait=[("tS", ts2)],
                    inc=("tV", 1))
            tg = op("vector", "tensor_scalar", s_mean4[:, 0:1],
                    s_mean4[:, 0:1], -1.0, None, ALU.mult,
                    wait=[("tV", tg)], inc=("tV", 1))
            ts3 = op("scalar", "activation", s_n3, s_y3, AF.Identity,
                     bias=s_mean4[:, 0:1], scale=s_rstd4[:, 0:1],
                     wait=[("tV", tg)], inc=("tS", 1))
            prev_ts3 = ts3
            wadd = [("tS", ts3), (sem3, t_embk[k])]
            if prev_st2[k % 2]:
                wadd.append((("dG", "dV")[k % 2], prev_st2[k % 2]))
            prev_add2[k % 2] = op("vector", "tensor_tensor", s_x1t3d[k % 2],
                                  s_emb3d[k % 2], s_n3, ALU.add, wait=wadd,
                                  inc=("tV", 1))
            prev_st2[k % 2] = op("gpsimd", "dma_start",
                                 out=d_x1[row:row + 128, :],
                                 in_=s_x1t3d[k % 2],
                                 wait=[("tV", prev_add2[k % 2])],
                                 inc=(("dG", "dV")[k % 2], 16))
            for _ in range(3):
                p4b_step()
        if "x1" in tap:
            op("gpsimd", "dma_start", out=tap["x1"], in_=d_x1,
               wait=[("dG", at("dG"))], inc=("dG", 16))

    if phases >= 4:
        for _ in range(120):
            p4a_step()
        if "q1tt" in tap:
            op("gpsimd", "dma_start", out=tap["q1tt"], in_=d_q1t,
               wait=[("dG", at("dG"))], inc=("dG", 16))
        if "tau" in tap:
            op("gpsimd", "dma_start", out=tap["tau"][:, 0:2], in_=s_tau[:, 0:2],
               wait=[("tV", at("tV"))], inc=("dG", 16))

        barrier()
        p4b_iter[0] = p4b_gen()
        for _ in range(130):
            p4b_step()
        if "q2t" in tap:
            op("gpsimd", "dma_start", out=tap["q2t"], in_=s_q2t,
               wait=[("tS", at("tS"))], inc=("dG", 16))

    if phases >= 5:
        barrier()
        t_b2l = op("sync", "dma_start", out=s_b2row, in_=dw["b2"][None, :],
                   inc=("dS", 16))
        w = bank_wait(ps_c, [("dS", t_b2l)])
        for half in range(2):
            op("tensor", "matmul", ps_c[:, half * 512:(half + 1) * 512],
               s_ones_row_f32, s_b2row[:, half * 512:(half + 1) * 512],
               start=True, stop=True, wait=w if half == 0 else (),
               inc=("tT", 1) if half == 1 else None)
            w = []
        t_b2 = op("scalar", "copy", s_b2b, ps_c, wait=[("tT", at("tT"))],
                  inc=("tS", 1))
        bank_set(ps_c, "tS", t_b2)

        prev_hev = [0] * 32
        prev_out_guard = [0, 0]
        prev_dm5 = [0]
        prev_cast5 = [0]
        tx1_hist = [None, None]
        prev_x1sem = [0, 0]
        q1t_tq = [0] * 32
        q1rez_tq = [0] * 32
        q1sems = ("dS", "dA", "dB", "dC")
        tout_hist = [[0] * 4, [0] * 4]
        t_ffn2mm = [0, 0]
        def emit_x1_in(ch):
            if ch >= 4:
                return
            x1b = s_x1q2[ch % 2]
            tx1 = []
            for i in range(4):
                row = (ch * 4 + i) * 128
                w = []
                if tout_hist[ch % 2][i]:
                    w.append(("tV", tout_hist[ch % 2][i]))
                if prev_x1sem[i % 2]:
                    w.append((("dX", "dY")[i % 2], prev_x1sem[i % 2]))
                prev_x1sem[i % 2] = op(("sync", "scalar")[i % 2], "dma_start",
                                       out=x1b[:, i * 1024:(i + 1) * 1024],
                                       in_=d_x1[row:row + 128, :],
                                       inc=(("dX", "dY")[i % 2], 16), wait=w)
                tx1.append(prev_x1sem[i % 2])
            tx1_hist[ch % 2] = tx1

        barrier()
        emit_x1_in(0)
        for ch in range(4):
            x1b = s_x1q2[ch % 2]
            a2b = s_a2t5x[ch % 2]
            tx1 = tx1_hist[ch % 2]
            prev_dm_c = prev_dm5[0]
            prev_cast_c = prev_cast5[0]
            for g2 in range(2):
                x0 = x1b[:, (g2 * 2) * 1024:(g2 * 2 + 1) * 1024]
                x1 = x1b[:, (g2 * 2 + 1) * 1024:(g2 * 2 + 2) * 1024]
                op("vector", None, wait=[("dX", tx1[2]), ("dY", tx1[3])])
                t_rstd = ln_stats_pair(x0, x1, prev_cast_c)
                prev_cast_c, prev_dm_c = ln_transpose_pair(
                    x0, x1, s_xc5, g2 * 256, t_rstd, a2b, 512, prev_dm_c)
            prev_dm5[0] = prev_dm_c
            prev_cast5[0] = prev_cast_c
            t_a2 = at("tS")
            if "a2t" in tap and ch == 0:
                op("gpsimd", "dma_start", out=tap["a2t"], in_=a2b,
                   wait=[("tS", t_a2)], inc=("dG", 16))
            for fp in range(16):
                fbs = (2 * fp, 2 * fp + 1)
                tqs = []
                for fb in fbs:
                    qbuf = s_q1rez[:, fb * 1024:(fb + 1) * 1024]
                    qsem = q1sems[fb % 4]
                    if ch > 0:
                        tqs.append((qsem, q1rez_tq[fb]))
                        continue
                    qeng = ("sync", "scalar")[fb % 2]
                    w = [("tS", prev_hev[fb - 4])] if fb >= 4 else []
                    tq = op(qeng, "dma_start",
                            out=qbuf.rearrange("e (ec f) -> e ec f", ec=8),
                            in_=d_q1t[fb].rearrange("ec e f -> e ec f"),
                            inc=(qsem, 16), wait=w)
                    q1rez_tq[fb] = tq
                    tqs.append((qsem, tq))
                ws = [bank_wait(ps_a, [tqs[0], ("tS", t_a2)]),
                      bank_wait(ps_b, [tqs[1], ("tS", t_a2)])]
                tmm2 = [0, 0]
                for ec in range(8):
                    for j in range(2):
                        fb = fbs[j]
                        bank = ps_a if j == 0 else ps_b
                        qb5 = s_q1rez[:, fb * 1024:(fb + 1) * 1024]
                        tmm2[j] = op("tensor", "matmul", bank,
                                     qb5[:, ec * 128:(ec + 1) * 128],
                                     a2b[:, ec * 512:(ec + 1) * 512],
                                     start=(ec == 0), stop=(ec == 7),
                                     wait=ws[j] if ec == 0 else (),
                                     inc=("tT", 1) if ec == 7 else None)
                        ws[j] = []
                for j in range(2):
                    fb = fbs[j]
                    bank = ps_a if j == 0 else ps_b
                    wev = [("tT", tmm2[j])]
                    if fb == 0 and t_ffn2mm[0]:
                        wev.append(("tT", t_ffn2mm[0]))
                    prev_hev[fb] = op("scalar", "activation",
                                      s_ht[:, fb * 512:(fb + 1) * 512], bank,
                                      AF.Relu, bias=s_b1t[:, fb:fb + 1],
                                      wait=wev, inc=("tS", 1))
                    bank_set(bank, "tS", prev_hev[fb])
            emit_x1_in(ch + 1)
            t_h = at("tS")
            if "ht" in tap and ch == 0:
                op("gpsimd", "dma_start", out=tap["ht"], in_=s_ht,
                   wait=[("tS", t_h)], inc=("dG", 16))
            for sb in range(4):
                bank = ps_c if sb % 2 == 0 else ps_d
                fbuf = s_f5 if sb % 2 == 0 else s_f5b
                obuf = s_o5 if sb % 2 == 0 else s_o5b
                w = bank_wait(bank, [("tS", t_h)])
                for fc in range(32):
                    lhs = s_ht[:, fc * 512 + sb * 128:
                               fc * 512 + (sb + 1) * 128]
                    op("tensor", "matmul", bank[:, 0:512], lhs,
                       s_q2t[:, fc * 1024: fc * 1024 + 512],
                       start=(fc == 0), stop=(fc == 31),
                       wait=w if fc == 0 else ())
                    w = []
                    op("tensor", "matmul", bank[:, 512:1024], lhs,
                       s_q2t[:, fc * 1024 + 512:(fc + 1) * 1024],
                       start=(fc == 0), stop=(fc == 31),
                       inc=("tT", 1) if fc == 31 else None)
                wf = [("tT", at("tT")), ("tS", t_b2)]
                if prev_out_guard[sb % 2]:
                    wf.append(("tV", prev_out_guard[sb % 2]))
                tfv = op("vector", "tensor_tensor", fbuf, bank, s_b2b,
                         ALU.add, wait=wf, inc=("tV", 1))
                bank_set(bank, "tV", tfv)
                op("vector", "bn_stats", s_stats[:, 24:30], fbuf[:, 0:512],
                   wait=[("tV", tfv)])
                tg = op("vector", "bn_stats", s_stats[:, 30:36],
                        fbuf[:, 512:1024], inc=("tV", 1))
                tg = op("vector", "bn_aggr", s_bn4[:, 4:6], s_stats[:, 24:36],
                        wait=[("tV", tg)], inc=("tV", 1))
                tg = op("vector", "tensor_scalar", s_rstd4[:, 2:3],
                        s_bn4[:, 5:6], EPS_LN, None, ALU.add,
                        wait=[("tV", tg)], inc=("tV", 1))
                ts1 = op("scalar", "activation", s_rstd4[:, 2:3],
                         s_rstd4[:, 2:3], AF.Ln, wait=[("tV", tg)],
                         inc=("tS", 1))
                ts2 = op("scalar", "activation", s_rstd4[:, 2:3],
                         s_rstd4[:, 2:3], AF.Exp, scale=-0.5,
                         wait=[("tS", ts1)], inc=("tS", 1))
                tg = op("vector", "tensor_tensor", s_mean4[:, 2:3],
                        s_bn4[:, 4:5], s_rstd4[:, 2:3], ALU.mult,
                        wait=[("tS", ts2)], inc=("tV", 1))
                tg = op("vector", "tensor_scalar", s_mean4[:, 2:3],
                        s_mean4[:, 2:3], -1.0, None, ALU.mult,
                        wait=[("tV", tg)], inc=("tV", 1))
                ts3 = op("scalar", "activation", obuf, fbuf, AF.Identity,
                         bias=s_mean4[:, 2:3], scale=s_rstd4[:, 2:3],
                         wait=[("tV", tg)], inc=("tS", 1))
                tout = op("vector", "tensor_tensor", obuf,
                          x1b[:, sb * 1024:(sb + 1) * 1024], obuf, ALU.add,
                          wait=[("tS", ts3)], inc=("tV", 1))
                tout_hist[ch % 2][sb] = tout
                row = (ch * 4 + sb) * 128
                t_st = op("gpsimd", "dma_start", out=d_out[row:row + 128, :],
                          in_=obuf, wait=[("tV", tout)], inc=("dG", 16))
                prev_out_guard[sb % 2] = op("vector", "engine_nop",
                                            wait=[("dG", t_st)],
                                            inc=("tV", 1))
            t_ffn2mm[0] = at("tT")
    else:
        op("gpsimd", "dma_start", out=d_out[0:128, :],
           in_=big1.bitcast(F32)[:, 0:1024],
           wait=[("tS", at("tS")), ("tV", at("tV"))], inc=("dG", 16))

    barrier(("gpsimd",))
    pr.emit()
    return nc, pr.nops


def kernel(L0_attn_emb, L0_attn_W1, L0_attn_W2, L0_attn_W3, L0_attn_Wo,
           L0_ffn_W1, L0_ffn_b1, L0_ffn_W2, L0_ffn_b2):
    C = host_constants()
    nc, _ = build_nc(taps=TAPS, phases=PHASES)
    emb = np.ascontiguousarray(np.asarray(L0_attn_emb, dtype=np.float32))
    base = {
        "w1": np.ascontiguousarray(L0_attn_W1, dtype=np.float32),
        "w2": np.ascontiguousarray(L0_attn_W2, dtype=np.float32),
        "w3": np.ascontiguousarray(L0_attn_W3, dtype=np.float32),
        "wo": np.ascontiguousarray(L0_attn_Wo, dtype=np.float32),
        "wf1": np.ascontiguousarray(L0_ffn_W1, dtype=np.float32),
        "b1": np.ascontiguousarray(L0_ffn_b1, dtype=np.float32),
        "wf2": np.ascontiguousarray(L0_ffn_W2, dtype=np.float32),
        "b2": np.ascontiguousarray(L0_ffn_b2, dtype=np.float32),
    }
    base.update(C)
    in_maps = [dict(base, emb=emb[b]) for b in range(8)]
    res = run_bass_kernel_spmd(nc, in_maps, core_ids=list(range(8)))
    return np.stack([np.asarray(r["out"]) for r in res.results])


# version 53
# speedup vs baseline: 1.1057x; 1.1057x over previous
import math
import numpy as np
import ml_dtypes
import concourse.bass as bass
import concourse.mybir as mybir
from concourse.bass_utils import run_bass_kernel_spmd

F32 = mybir.dt.float32
BF16 = mybir.dt.bfloat16
F8 = mybir.dt.float8e4
AF = mybir.ActivationFunctionType
ALU = mybir.AluOpType
AX = mybir.AxisListType
BF = ml_dtypes.bfloat16

S, E, P, H, D, FF = 2048, 1024, 1024, 8, 128, 4096
EPS_LN = 1e-5
EPS_Q = 1e-5
ISQRT8 = float(1.0 / math.sqrt(8.0))

TAPS = ()
PHASES = 99


def host_constants():
    iden = np.eye(128, dtype=BF)
    Prot = np.zeros((128, 128), np.float32)
    for i in range(64):
        Prot[2 * i, 2 * i + 1] = -1.0
        Prot[2 * i + 1, 2 * i] = 1.0
    invf2 = np.repeat(1.0 / 10000.0 ** (np.arange(0, D, 2) / D), 2)
    u = np.arange(256, dtype=np.float64)
    cosT = np.empty((128, 2048), np.float64)
    sinT = np.empty((128, 2048), np.float64)
    for c in range(8):
        ang = (u * 8 + c)[None, :] * invf2[:, None]
        cosT[:, c * 256:(c + 1) * 256] = np.cos(ang)
        sinT[:, c * 256:(c + 1) * 256] = np.sin(ang)
    return {"c_iden": iden,
            "c_ones_row_bf": np.ones((1, 128), dtype=BF),
            "c_ones_row_f32": np.ones((1, 128), dtype=np.float32),
            "c_ones_col_bf": np.ones((128, 1), dtype=BF),
            "c_ones_col_f32": np.ones((128, 1), dtype=np.float32),
            "c_pt": np.ascontiguousarray(Prot.T).astype(BF),
            "c_iden_f32": np.eye(128, dtype=np.float32),
            "c_cos": cosT.astype(BF), "c_sin": sinT.astype(BF)}


class Prog:
    ENGINES = ("sync", "scalar", "vector", "tensor", "gpsimd")
    SEMS = ("dS", "dG", "dA", "tT", "tV", "tS", "tG", "dV",
            "dB", "dC", "dX", "dY", "dZ")

    def __init__(self, nc):
        self.nc = nc
        self.plan = {e: [] for e in self.ENGINES}
        self.sem = {}
        self.cnt = {}
        self.nops = 0
        for s in self.SEMS:
            self.sem[s] = nc.semaphore(s).__enter__()
            self.cnt[s] = 0

    def op(self, eng, meth, *args, wait=(), inc=None, **kw):
        self.plan[eng].append((meth, args, kw, tuple(wait), inc))
        self.nops += 1
        if inc is not None:
            self.cnt[inc[0]] += inc[1]
            return self.cnt[inc[0]]
        return None

    def at(self, sem):
        return self.cnt[sem]

    def barrier(self, engs=None):
        ws = tuple((s, self.cnt[s]) for s in self.SEMS if self.cnt[s] > 0)
        for e in (engs or self.ENGINES):
            self.plan[e].append((None, (), {}, ws, None))

    def emit(self):
        blk = self.nc.Block().__enter__()
        for eng in self.ENGINES:
            items = self.plan[eng]
            if not items:
                continue

            def body(e, items=items):
                for meth, args, kw, waits, inc in items:
                    for sname, v in waits:
                        e.wait_ge(self.sem[sname], v)
                    if meth is None:
                        continue
                    ins = getattr(e, meth)(*args, **kw)
                    if inc is not None:
                        ins.then_inc(self.sem[inc[0]], inc[1])
            getattr(blk, eng)(body)
        blk.__exit__(None, None, None)


def build_nc(taps=(), phases=99):
    nc = bass.Bass("TRN2", target_bir_lowering=False, debug=False)
    pr = Prog(nc)
    op, at, barrier = pr.op, pr.at, pr.barrier
    C = host_constants()

    d_emb = nc.dram_tensor("emb", [S, E], F32, kind="ExternalInput").ap()
    dw = {n: nc.dram_tensor(n, shp, F32, kind="ExternalInput").ap()
          for n, shp in (("w1", [P, E]), ("w2", [P, E]), ("w3", [P, E]),
                         ("wo", [E, P]), ("wf1", [FF, E]), ("b1", [FF]),
                         ("wf2", [E, FF]), ("b2", [E]))}
    d_out = nc.dram_tensor("out", [S, E], F32, kind="ExternalOutput").ap()
    dc = {k: nc.dram_tensor(k, list(v.shape),
                            BF16 if v.dtype == BF else F32,
                            kind="ExternalInput").ap() for k, v in C.items()}
    d_x1 = nc.dram_tensor("x1s", [S, E], F32, kind="Internal").ap()
    d_q1t = nc.dram_tensor("q1ts", [32, 128, 1024], F8, kind="Internal").ap()

    TS = {"wt1": ([128, 8192], BF16), "wtop": ([128, 8192], BF16),
          "at": ([128, 16384], BF16), "qband": ([128, 4096], BF16),
          "kband": ([128, 4096], BF16), "vband": ([128, 4096], BF16),
          "eb": ([128, 8192], BF16), "attn": ([128, 16384], BF16),
          "x1": ([S, E], F32), "a2t": ([128, 4096], BF16),
          "ht": ([128, 16384], BF16), "q2t": ([128, 32768], BF16),
          "q1tt": ([32, 8, 128, 128], BF16), "mhat": ([128, 32], F32),
          "sig": ([128, 2048], F32), "otn": ([128, 2048], BF16),
          "tau": ([128, 8], F32), "xcdbg": ([128, 2048], BF16),
          "dmdbg": ([128, 256], BF16), "scdbg": ([128, 8], F32)}
    tap = {t: nc.dram_tensor("tap_" + t, list(TS[t][0]), TS[t][1],
                             kind="ExternalOutput").ap() for t in taps}

    A = {}

    def sb(name, shape, dtype):
        apx = nc.alloc_sbuf_tensor(name, shape, dtype).ap()
        A[name] = apx
        return apx

    big1 = sb("big1", [128, 32768], BF16)
    s_at = big1[:, 0:16384]
    s_attn = big1[:, 16384:32768]
    s_q2t = big1.bitcast(F8)[:, 0:32768]
    s_q1rez = big1.bitcast(F8)[:, 32768:65536]

    big2 = sb("big2", [128, 24576], BF16)
    s_wt1 = big2[:, 0:8192]
    s_wt2 = big2[:, 8192:16384]
    s_wt3 = big2[:, 16384:24576]
    s_ht = big2[:, 0:16384]
    s_x1q = big2[:, 16384:24576].bitcast(F32)

    big3 = sb("big3", [128, 8192], BF16)
    s_wtop = big3
    s_q1d = (big3[:, 0:1024], big3[:, 1024:2048])
    s_a2t5 = big3[:, 2048:6144]
    s_o5b = big3[:, 6144:8192].bitcast(F32)

    big4 = sb("big4", [128, 8192], BF16)
    s_wnat = big4
    s_et = big4
    s_f5 = big4[:, 0:2048].bitcast(F32)
    s_f5b = big4[:, 2048:4096].bitcast(F32)
    s_b2b = big4[:, 4096:6144].bitcast(F32)
    s_o5 = big4[:, 6144:8192].bitcast(F32)

    big5 = sb("big5", [128, 4096], BF16)
    s_xt = big5.bitcast(F32)
    s_sq = big5
    s_y3 = big5[:, 0:2048].bitcast(F32)
    s_n3 = big5[:, 2048:4096].bitcast(F32)
    s_wfs = big5.bitcast(F32)

    big6 = sb("big6", [128, 4096], BF16)
    s_xc = big6[:, 0:2048]
    s_abq = big6.bitcast(F32)[:, 0:1024]

    s_xc5 = big6[:, 0:2048]

    big7 = sb("big7", [128, 12288], BF16)
    s_qb = big7[:, 0:4096]
    s_kb = big7[:, 4096:8192]
    s_vb = big7[:, 8192:12288]
    s_q2s = big7[:, 0:8192]
    s_q2s2 = (big7[:, 0:8192], big2[:, 0:8192])
    s_wfs4b = big7.bitcast(F32)[:, 4096:6144]
    s_q1stage = big7[:, 8192:10240]

    big8 = sb("big8", [128, 6144], BF16)
    s_sg = big8[:, 0:2048]
    s_otn = big8[:, 2048:4096]
    s_rsigb = big8[:, 4096:5120].bitcast(F32)
    s_rsig = big8[0:1, 5120:6144].bitcast(F32)
    s_x1t3 = big8[:, 4096:6144].bitcast(F32)
    s_q1d4 = (big5.bitcast(F8)[:, 0:1024], big5.bitcast(F8)[:, 1024:2048],
              big5.bitcast(F8)[:, 2048:3072],
              big5.bitcast(F8)[:, 3072:4096])
    s_x1q2 = (s_x1q, big7[:, 2048:10240].bitcast(F32))
    s_a2t5x = (s_a2t5, big8[:, 0:4096])
    s_emb3d = (big8.bitcast(F32)[:, 0:1024],
               big8.bitcast(F32)[:, 1024:2048])
    s_x1t3d = (s_x1t3, big6.bitcast(F32)[:, 1024:2048])

    s_iden = sb("s_iden", [128, 128], BF16)
    s_iden_f32 = sb("s_iden_f32", [128, 128], F32)
    s_b1s = sb("s_b1s", [32, 128], F32)
    s_pt = sb("s_pt", [128, 128], BF16)
    s_ones_row_bf = sb("s_ones_row_bf", [1, 128], BF16)
    s_ones_row_f32 = sb("s_ones_row_f32", [1, 128], F32)
    s_ones_col_bf = sb("s_ones_col_bf", [128, 1], BF16)
    s_ones_col_f32 = sb("s_ones_col_f32", [128, 1], F32)
    s_cos = sb("s_cos", [128, 2048], BF16)
    s_sin = sb("s_sin", [128, 2048], BF16)
    s_stats = sb("s_stats", [128, 40], F32)
    s_bn4 = sb("s_bn4", [128, 8], F32)
    s_mean4 = sb("s_mean4", [128, 4], F32)
    s_rstd4 = sb("s_rstd4", [128, 4], F32)
    s_dmat = sb("s_dmat", [128, 256], BF16)
    s_nm = sb("s_nm", [128, 32], F32)
    s_eb = sb("s_eb", [128, 1], F32)
    s_b1t = sb("s_b1t", [128, 32], F32)
    s_b2row = s_b2b[0:1, :]
    s_gs = sb("s_gs", [128, 32], F32)
    s_tau = sb("s_tau", [128, 4], F32)

    per_part = sum(a.tensor.shape[1] * (2 if a.dtype == BF16 else 4)
                   for a in A.values())
    assert per_part <= 212 * 1024, per_part

    ps_a = nc.alloc_psum_tensor("ps_a", [128, 512], F32).ap()
    ps_b = nc.alloc_psum_tensor("ps_b", [128, 512], F32).ap()
    ps_c = nc.alloc_psum_tensor("ps_c", [128, 1024], F32).ap()
    ps_d = nc.alloc_psum_tensor("ps_d", [128, 1024], F32).ap()
    ps_g = nc.alloc_psum_tensor("ps_g", [128, 1024], F32).ap()
    ps_e = ps_a
    ps_f = ps_b
    bg = {}

    def bank_wait(b, extra=()):
        w = list(extra)
        if id(b.tensor) in bg:
            s, v = bg[id(b.tensor)]
            w.append((s, v))
        return w

    def bank_set(b, sem, v):
        bg[id(b.tensor)] = (sem, v)

    for k, dst in (("c_iden", s_iden), ("c_pt", s_pt),
                   ("c_ones_row_bf", s_ones_row_bf),
                   ("c_ones_row_f32", s_ones_row_f32),
                   ("c_ones_col_bf", s_ones_col_bf),
                   ("c_ones_col_f32", s_ones_col_f32),
                   ("c_iden_f32", s_iden_f32),
                   ("c_cos", s_cos), ("c_sin", s_sin)):
        op("sync", "dma_start", out=dst, in_=dc[k], inc=("dS", 16))
    op("sync", "dma_start", out=s_b1s,
       in_=dw["b1"].rearrange("(t p) -> t p", p=128), inc=("dS", 16))
    t_consts = at("dS")
    tmm = op("tensor", "matmul", ps_f[:, 0:32], s_b1s, s_iden_f32[0:32, 0:32],
             start=True, stop=True, wait=[("dS", t_consts)], inc=("tT", 1))
    tb1 = op("scalar", "copy", s_b1t, ps_f[:, 0:32], wait=[("tT", tmm)],
             inc=("tS", 1))
    bank_set(ps_f, "tS", tb1)

    def transpose_groups(groups, wait_first):
        first = True
        for gi, (lhss, dst) in enumerate(groups):
            bank = ps_a if gi % 2 == 0 else ps_b
            w = bank_wait(bank, wait_first if first else ())
            first = False
            n = len(lhss)
            for q, lhs in enumerate(lhss):
                op("tensor", "matmul", bank[:, q * 128:(q + 1) * 128], lhs,
                   s_iden, start=True, stop=True, wait=w if q == 0 else (),
                   inc=("tT", 1) if q == n - 1 else None)
                w = []
            tmm = at("tT")
            tv = op("scalar", "copy", dst, bank[:, 0:128 * n],
                    wait=[("tT", tmm)], inc=("tS", 1))
            bank_set(bank, "tS", tv)

    s_wnat2 = big1[:, 16384:24576]

    def wnat_load(d_w, buf, sem, wv):
        src = d_w.rearrange("(t p) e -> p t e", p=128)
        return op("gpsimd", "dma_start",
                  out=buf.rearrange("p (t e) -> p t e", t=8), in_=src,
                  inc=(sem, 16), wait=wv)

    def wnat_transposes(buf, s_dst, wait_first, permute=False):
        groups = []
        for ec in range(8):
            for half in range(2):
                if permute:
                    lhss = [buf[:, rb * 1024 + ec: rb * 1024 + 1024: 8]
                            for rb in range(half * 4, half * 4 + 4)]
                else:
                    lhss = [buf[:, pb * 1024 + ec * 128:
                                pb * 1024 + (ec + 1) * 128]
                            for pb in range(half * 4, half * 4 + 4)]
                groups.append((lhss, s_dst[:, ec * 1024 + half * 512:
                                           ec * 1024 + (half + 1) * 512]))
        transpose_groups(groups, wait_first)
        return at("tT")

    if phases >= 0:
        tl1 = wnat_load(dw["w1"], s_wnat, "dG", [("dS", t_consts)])
        tl2 = wnat_load(dw["w2"], s_wnat2, "dZ", [])
        tf1 = wnat_transposes(s_wnat, s_wt1, [("dG", tl1)])
        tl3 = wnat_load(dw["w3"], s_wnat, "dG", [("tT", tf1)])
        tf2 = wnat_transposes(s_wnat2, s_wt2, [("dZ", tl2)])
        tl4 = wnat_load(dw["wo"], s_wnat2, "dZ", [("tT", tf2)])
        wnat_transposes(s_wnat, s_wt3, [("dG", tl3)])
        wnat_transposes(s_wnat2, s_wtop, [("dZ", tl4)], permute=True)
        if "wt1" in tap:
            op("gpsimd", "dma_start", out=tap["wt1"], in_=s_wt1,
               wait=[("tS", at("tS"))], inc=("dG", 16))
        if "wtop" in tap:
            op("gpsimd", "dma_start", out=tap["wtop"], in_=s_wtop,
               wait=[("tS", at("tS"))], inc=("dG", 16))

    def ln_stats_pair(x0, x1, prev_cast_tS):
        w = [("tS", prev_cast_tS)] if prev_cast_tS else []
        op("vector", "bn_stats", s_stats[:, 0:6], x0[:, 0:512], wait=w)
        op("vector", "bn_stats", s_stats[:, 6:12], x0[:, 512:1024])
        op("vector", "bn_stats", s_stats[:, 12:18], x1[:, 0:512])
        tg = op("vector", "bn_stats", s_stats[:, 18:24], x1[:, 512:1024],
                inc=("tV", 1))
        op("vector", "bn_aggr", s_bn4[:, 0:2], s_stats[:, 0:12],
           wait=[("tV", tg)])
        tg = op("vector", "bn_aggr", s_bn4[:, 2:4], s_stats[:, 12:24],
                inc=("tV", 1))
        op("vector", "tensor_scalar", s_mean4[:, 0:2], s_bn4[:, 0:4:2],
           -1.0, None, ALU.mult, wait=[("tV", tg)])
        tg = op("vector", "tensor_scalar", s_rstd4[:, 0:2], s_bn4[:, 1:4:2],
                EPS_LN, None, ALU.add, inc=("tV", 1))
        ts1 = op("scalar", "activation", s_rstd4[:, 0:2], s_rstd4[:, 0:2],
                 AF.Ln, wait=[("tV", tg)], inc=("tS", 1))
        ts2 = op("scalar", "activation", s_rstd4[:, 0:2], s_rstd4[:, 0:2],
                 AF.Exp, scale=-0.5, wait=[("tS", ts1)], inc=("tS", 1))
        return ts2

    def ln_transpose_pair(x0, x1, xc, dst_cols, t_rstd, dst, dst_stride,
                          prev_dmat_tT):
        tcast = 0
        for i, x in enumerate((x0, x1)):
            tcast = op("scalar", "activation", xc[:, i * 1024:(i + 1) * 1024],
                       x, AF.Identity, bias=s_mean4[:, i:i + 1],
                       wait=[("tS", t_rstd)] if i == 0 else (),
                       inc=("tS", 1))
        w = [("tS", t_rstd)]
        if prev_dmat_tT:
            w.append(("tT", prev_dmat_tT))
        td = 0
        for i in range(2):
            td = op("vector", "tensor_scalar",
                    s_dmat[:, i * 128:(i + 1) * 128],
                    s_iden, s_rstd4[:, i:i + 1], None, ALU.mult,
                    wait=w if i == 0 else (), inc=("tV", 1))
            w = []
        last = 0
        for ec in range(8):
            bank = ps_a if ec % 2 == 0 else ps_b
            w = bank_wait(bank, [("tS", tcast), ("tV", td)])
            for i in range(2):
                op("tensor", "matmul", bank[:, i * 128:(i + 1) * 128],
                   xc[:, i * 1024 + ec * 128: i * 1024 + (ec + 1) * 128],
                   s_dmat[:, i * 128:(i + 1) * 128], start=(i == 0),
                   stop=(i == 1),
                   wait=w if i == 0 else (),
                   inc=("tT", 1) if i == 1 else None)
                w = []
            tmm = at("tT")
            tv = op("scalar", "copy",
                    dst[:, ec * dst_stride + dst_cols: ec * dst_stride + dst_cols + 256],
                    bank[:, 0:256], wait=[("tT", tmm)], inc=("tS", 1))
            bank_set(bank, "tS", tv)
            last = tv
        return last, at("tT")

    if phases >= 1:
        xt4 = (big5.bitcast(F32)[:, 0:1024], big5.bitcast(F32)[:, 1024:2048],
               big8.bitcast(F32)[:, 0:1024], big8.bitcast(F32)[:, 1024:2048])
        xsems = ("dS", "dA", "dB", "dC")
        prev_cast2 = [0, 0]
        prev_dm = 0
        prev_cast = 0
        for g in range(8):
            p = g % 2
            tl = []
            for i in range(2):
                t = g * 2 + i
                n4 = p * 2 + i
                wld = ([("tS", prev_cast2[p])] if prev_cast2[p]
                       else [("dS", t_consts)])
                tl.append(op(("sync", "scalar")[i], "dma_start", out=xt4[n4],
                             in_=d_emb[t * 128:(t + 1) * 128, :],
                             wait=wld, inc=(xsems[n4], 16)))
            x0 = xt4[p * 2]
            x1 = xt4[p * 2 + 1]
            op("vector", None, wait=[(xsems[p * 2], tl[0]),
                                     (xsems[p * 2 + 1], tl[1])])
            t_rstd = ln_stats_pair(x0, x1, 0)
            prev_cast, prev_dm = ln_transpose_pair(
                x0, x1, s_xc, g * 256, t_rstd, s_at, 2048, prev_dm)
            prev_cast2[p] = prev_cast
            if g == 1 and "xcdbg" in tap:
                op("gpsimd", "dma_start", out=tap["xcdbg"], in_=s_xc,
                   wait=[("tS", prev_cast)], inc=("dG", 16))
                op("gpsimd", "dma_start", out=tap["dmdbg"], in_=s_dmat,
                   wait=[("tS", prev_cast)], inc=("dG", 16))
                op("gpsimd", "dma_start", out=tap["scdbg"][:, 0:2],
                   in_=s_mean4, wait=[("tS", prev_cast)], inc=("dG", 16))
                op("gpsimd", "dma_start", out=tap["scdbg"][:, 2:4],
                   in_=s_rstd4, wait=[("tS", prev_cast)], inc=("dG", 16))
                op("gpsimd", "dma_start", out=tap["scdbg"][:, 4:8],
                   in_=s_bn4, wait=[("tS", prev_cast)], inc=("dG", 16))
                barrier()
        t_p1_done = at("tS")
        if "at" in tap:
            op("gpsimd", "dma_start", out=tap["at"], in_=s_at,
               wait=[("tS", t_p1_done)], inc=("dG", 16))

    p4a_wfsA = big5.bitcast(F32)[:, 0:1024]
    p4a_wfsB = big5.bitcast(F32)[:, 1024:2048]
    p4a_stage = big5.bitcast(F8)[:, 4096:6144]
    p4a_A = big6[:, 0:1024]
    p4a_B = big6[:, 1024:2048]
    p4a_pair = big6[:, 2048:4096]

    def p4a_gen(first_wait):
        src_w1 = dw["wf1"].rearrange("(t p) e -> t p e", p=128)
        red = [0] * 32
        for t in range(32):
            buf = p4a_wfsA if t % 2 == 0 else p4a_wfsB
            qeng, qsem = ("sync", "dS") if t % 2 == 0 else ("scalar", "dA")
            w = [("tV", red[t - 2])] if t >= 2 else list(first_wait)
            tl = op(qeng, "dma_start", out=buf, in_=src_w1[t], inc=(qsem, 16),
                    wait=w)
            red[t] = op("vector", "reduce_sum", s_stats[:, t:t + 1], buf,
                        axis=AX.X, apply_absolute_value=True,
                        wait=[(qsem, tl)], inc=("tV", 1))
            yield
        tg = op("vector", "reduce_sum", s_stats[:, 32:33], s_stats[:, 0:32],
                axis=AX.X, wait=[("tV", red[31])], inc=("tV", 1))
        w = bank_wait(ps_f, [("tV", tg)])
        tmm = op("tensor", "matmul", ps_f[0:1, 0:1], s_ones_col_f32,
                 s_stats[:, 32:33], start=True, stop=True, wait=w,
                 inc=("tT", 1))
        tg = op("vector", "tensor_scalar", s_stats[0:1, 33:34],
                ps_f[0:1, 0:1], 0.5 / (FF * E), 0.5 * EPS_Q, ALU.mult,
                op1=ALU.add, wait=[("tT", tmm)], inc=("tV", 1))
        bank_set(ps_f, "tV", tg)
        w = bank_wait(ps_f, [("tV", tg)])
        tmm = op("tensor", "matmul", ps_f[:, 1:2], s_ones_row_f32,
                 s_stats[0:1, 33:34], start=True, stop=True, wait=w,
                 inc=("tT", 1))
        ts = op("scalar", "copy", s_tau[:, 0:1], ps_f[:, 1:2],
                wait=[("tT", tmm)], inc=("tS", 1))
        bank_set(ps_f, "tS", ts)
        t_tau = op("vector", "tensor_scalar", s_tau[:, 1:2], s_tau[:, 0:1],
                   -1.0, None, ALU.mult, wait=[("tS", ts)], inc=("tV", 1))
        yield
        prev_q = 0
        prev_tp = 0
        prev_st = 0
        stage4 = p4a_stage.rearrange("p (ec q f) -> p ec q f", ec=8, q=2)
        for gp in range(16):
            for i in range(2):
                ft = gp * 2 + i
                qeng, qsem = ("sync", "dS") if i == 0 else ("scalar", "dA")
                w = [("tV", prev_q if gp or i else t_tau)]
                tl = op(qeng, "dma_start", out=p4a_wfsA,
                        in_=src_w1[ft], inc=(qsem, 16), wait=w)
                wv = [(qsem, tl), ("tT", prev_tp)]
                prev_q = quantize_p4a(p4a_pair[:, i * 1024:(i + 1) * 1024],
                                      p4a_wfsA, wv)
                if i == 0:
                    yield
            for ec in range(8):
                bankp = ps_f if ec % 2 == 0 else ps_e
                w = bank_wait(bankp, [("tV", prev_q)] if ec == 0 else ())
                if ec == 0 and prev_st:
                    w.append(("dG", prev_st))
                for i in range(2):
                    op("tensor", "matmul",
                       bankp[:, i * 128:(i + 1) * 128],
                       p4a_pair[:, i * 1024 + ec * 128:
                                i * 1024 + (ec + 1) * 128],
                       s_iden, start=(i == 0), stop=(i == 1),
                       wait=w if i == 0 else (),
                       inc=("tT", 1) if i == 1 else None)
                    w = []
                tv = op("scalar", "copy", p4a_stage[:, ec * 256:(ec + 1) * 256],
                        bankp[:, 0:256],
                        wait=[("tT", at("tT"))], inc=("tS", 1))
                bank_set(bankp, "tS", tv)
                if ec % 2 == 1:
                    yield
            prev_tp = at("tT")
            tev = at("tS")
            for i in range(2):
                ft = gp * 2 + i
                prev_st = op("gpsimd", "dma_start",
                             out=d_q1t[ft].rearrange("e (ec f) -> e ec f",
                                                     ec=8),
                             in_=stage4[:, :, i, :], inc=("dG", 16),
                             wait=[("tS", tev)] if i == 0 else ())
            yield
        while True:
            yield

    def quantize_p4a(dst, srcb, wv):
        wv = list(wv)
        if q_chain[0]:
            wv.append(("tV", q_chain[0]))
        tq = op("vector", "tensor_single_scalar", p4a_B, srcb,
                s_tau[:, 1:2], ALU.is_lt, wait=wv, inc=("tV", 1))
        q_chain[0] = op("vector", "scalar_tensor_tensor", dst, srcb,
                        s_tau[:, 0:1], p4a_B, ALU.is_gt, ALU.subtract,
                        wait=[("tV", tq)], inc=("tV", 1))
        return q_chain[0]

    q_chain = [0]
    p4a_iter = [None]

    def p4a_step():
        if p4a_iter[0] is not None:
            next(p4a_iter[0])

    if phases >= 2:
        t_att = 0
        ebuf_guard = [0, 0]
        if phases >= 4:
            p4a_iter[0] = p4a_gen([("tS", t_p1_done)])
        pairs3 = (ps_c, ps_d, ps_g)
        s_e0 = big8[:, 0:2048]
        s_e1 = big8[:, 2048:4096]
        s_otn2 = big8[:, 4096:4608]
        s_otn2b = big8[:, 4608:5120]
        etv = s_et.rearrange("p (kc q) -> p kc q", kc=16)
        ebuf_of = (s_e0, s_e1)
        exp_done = [0, 0, 0]
        ebuf_guard = [0, 0]
        tpk = [0]
        ST = {}
        pend_group = [None]
        pend_ot = [None]
        grp_rs = []

        def u_scores(band, hh, qb, qg, kh):
            u = 2 * qb + kh
            bank = pairs3[u % 3]
            cq, qhalf = qg // 2, qg % 2
            lhs_q = s_qb[:, hh * 2048 + cq * 256 + qhalf * 128:
                         hh * 2048 + cq * 256 + (qhalf + 1) * 128]
            w = bank_wait(bank)
            if qb == 0 and kh == 0:
                w.append(("tV", rope_v_of[band]))
            for half in range(2):
                rhs = s_kb[:, hh * 2048 + kh * 1024 + half * 512:
                           hh * 2048 + kh * 1024 + (half + 1) * 512]
                op("tensor", "matmul", bank[:, half * 512:(half + 1) * 512],
                   lhs_q, rhs, start=True, stop=True,
                   wait=w if half == 0 else (),
                   inc=("tT", 1) if half == 1 else None)
                w = []
            return at("tT")

        def u_maxes(qb, kh, tmm):
            u = 2 * qb + kh
            c0 = (qb % 3) * 6
            bank = pairs3[u % 3]
            tmax = op("vector", "reduce_max", s_nm[:, c0 + kh:c0 + kh + 1],
                      bank, axis=AX.X, wait=[("tT", tmm)], inc=("tV", 1))
            if kh == 0:
                return op("vector", "tensor_scalar", s_nm[:, c0 + 3:c0 + 4],
                          s_nm[:, c0:c0 + 1], -ISQRT8, None, ALU.mult,
                          wait=[("tV", tmax)], inc=("tV", 1))
            tm = op("vector", "reduce_max", s_nm[:, c0 + 2:c0 + 3],
                    s_nm[:, c0:c0 + 2], axis=AX.X, wait=[("tV", tmax)],
                    inc=("tV", 1))
            tnb = op("vector", "tensor_scalar", s_nm[:, c0 + 4:c0 + 5],
                     s_nm[:, c0 + 2:c0 + 3], -ISQRT8, None, ALU.mult,
                     wait=[("tV", tm)], inc=("tV", 1))
            return op("vector", "scalar_tensor_tensor",
                      s_nm[:, c0 + 5:c0 + 6], s_nm[:, c0:c0 + 1], ISQRT8,
                      s_nm[:, c0 + 4:c0 + 5], ALU.mult, ALU.add,
                      wait=[("tV", tnb)], inc=("tV", 1))

        def u_exps(qb, kh, tnb):
            u = 2 * qb + kh
            bank = pairs3[u % 3]
            c0 = (qb % 3) * 6
            a0 = (qb % 2) * 2
            ebuf = ebuf_of[qb % 2]
            wx = [("tV", tnb)]
            if kh == 0:
                wx.append(("tT", ebuf_guard[qb % 2]))
            tse = op("scalar", "activation", ebuf[:, kh * 1024:(kh + 1) * 1024],
                     bank, AF.Exp,
                     bias=s_nm[:, c0 + 3 + kh:c0 + 4 + kh], scale=ISQRT8,
                     accum_out=s_gs[:, a0 + kh:a0 + kh + 1],
                     wait=wx, inc=("tS", 1))
            bank_set(bank, "tS", tse)
            exp_done[u % 3] = tse
            if kh == 1:
                st = ST[qb]
                st["tf0"] = op("scalar", "activation",
                               s_gs[:, 24 + qb % 2:25 + qb % 2],
                               s_nm[:, (qb % 3) * 6 + 5:(qb % 3) * 6 + 6],
                               AF.Exp, inc=("tS", 1))
                st["f0g"] = op("scalar", "copy", s_tau[0:1, 2:3],
                               s_nm[0:1, (qb % 3) * 6 + 4:(qb % 3) * 6 + 5],
                               inc=("tS", 1))
            return tse

        def u_sigma_d0(qb):
            st = ST[qb]
            a0 = (qb % 2) * 2
            tsg = op("vector", "scalar_tensor_tensor",
                     s_gs[:, 12 + qb % 4:13 + qb % 4], s_gs[:, a0:a0 + 1],
                     s_gs[:, 24 + qb % 2:25 + qb % 2],
                     s_gs[:, a0 + 1:a0 + 2], ALU.mult, ALU.add,
                     wait=[("tS", st["f0g"])], inc=("tV", 1))
            st["trs"] = op("vector", "reciprocal",
                           s_gs[:, 16 + qb % 8:17 + qb % 8],
                           s_gs[:, 12 + qb % 4:13 + qb % 4],
                           wait=[("tV", tsg)], inc=("tV", 1))
            st["td0"] = op("scalar", "activation",
                           s_dmat[:, (qb % 2) * 128:(qb % 2) * 128 + 128],
                           s_iden, AF.Identity,
                           scale=s_gs[:, 24 + qb % 2:25 + qb % 2],
                           wait=[("tS", st["tf0"]),
                                 ("tT", ebuf_guard[qb % 2])],
                           inc=("tS", 1))

        def u_transposes(qb, kh):
            st = ST[qb]
            qj = st["qj"]
            ebuf = ebuf_of[qb % 2]
            d0 = s_dmat[:, (qb % 2) * 128:(qb % 2) * 128 + 128]
            for kg in range(2):
                bank = ps_a if tpk[0] % 2 == 0 else ps_b
                tpk[0] += 1
                w = bank_wait(bank)
                if kg == 0 and kh == 0:
                    w += [("tS", st["td0"]), ("tS", exp_done[(2 * qb + 1) % 3])]
                for t4 in range(4):
                    kc = kh * 8 + kg * 4 + t4
                    tl4 = op("tensor", "matmul",
                             bank[:, t4 * 128:(t4 + 1) * 128],
                             ebuf[:, kc * 128:(kc + 1) * 128],
                             d0 if kc < 8 else s_iden,
                             start=(t4 == 0), stop=(t4 == 3),
                             wait=w if t4 == 0 else (),
                             inc=("tT", 1) if t4 == 3 else None)
                    w = []
                kg0 = kh * 2 + kg
                dstv = etv[:, kg0 * 4:kg0 * 4 + 4, qj * 128:(qj + 1) * 128]
                srcv = bank.rearrange("p (a b) -> p a b", a=4)
                if kg0 != 3:
                    tv = op("scalar", "copy", dstv, srcv,
                            wait=[("tT", at("tT"))], inc=("tS", 1))
                    bank_set(bank, "tS", tv)
                else:
                    tv = op("vector", "tensor_copy", dstv, srcv,
                            wait=[("tT", at("tT"))], inc=("tV", 1))
                    bank_set(bank, "tV", tv)
            if kh == 1:
                ebuf_guard[qb % 2] = tl4
                st["t_et"] = [("tS", at("tS")), ("tV", at("tV"))]

        def emit_group_av(gst):
            hh = gst["hh"]
            bankA = ps_a if tpk[0] % 2 == 0 else ps_b
            bankB = ps_b if tpk[0] % 2 == 0 else ps_a
            tpk[0] += 2
            ws = [bank_wait(bankA, list(gst["t_et"])), bank_wait(bankB)]
            tmm2 = [0, 0]
            for kk in range(8):
                for j in range(2):
                    kc = j * 8 + kk
                    bank = bankA if j == 0 else bankB
                    lhs = s_vb[:, (hh * 2 + kc % 2) * 1024 + (kc // 2) * 128:
                               (hh * 2 + kc % 2) * 1024 + (kc // 2 + 1) * 128]
                    tmm2[j] = op("tensor", "matmul", bank, lhs,
                                 s_et[:, kc * 512:(kc + 1) * 512],
                                 start=(kk == 0), stop=(kk == 7),
                                 wait=ws[j] if kk == 0 else (),
                                 inc=("tT", 1) if kk == 7 else None)
                    ws[j] = []
            tA = op("vector", "tensor_copy", s_otn2, bankA,
                    wait=[("tT", tmm2[0])], inc=("tV", 1))
            tB = op("vector", "tensor_copy", s_otn2b, bankB,
                    wait=[("tT", tmm2[1])], inc=("tV", 1))
            bank_set(bankA, "tV", tA)
            bank_set(bankB, "tV", tB)
            gst["t_oev"] = [("tV", tB)]

        def emit_group(gst):
            h, g = gst["h"], gst["g"]
            bank2 = ps_a if tpk[0] % 2 == 0 else ps_b
            tpk[0] += 1
            w = bank_wait(bank2, list(gst["t_oev"]))
            for qt in range(4):
                for srcb in (s_otn2, s_otn2b):
                    op("tensor", "matmul",
                       bank2[:, qt * 128:(qt + 1) * 128],
                       srcb[:, qt * 128:(qt + 1) * 128], s_iden,
                       start=(qt == 0 and srcb is s_otn2),
                       stop=(qt == 3 and srcb is s_otn2b),
                       wait=w if (qt == 0 and srcb is s_otn2) else (),
                       inc=("tT", 1) if (qt == 3 and srcb is s_otn2b)
                       else None)
                    w = []
            qid0 = gst["qb"] - 3
            for qt in range(4):
                tile = g * 4 + qt
                wq = [("tT", at("tT"))] if qt == 0 else []
                wq.append(("tV", gst["trs"][qt]))
                op("vector", "tensor_scalar",
                   s_attn[:, tile * 1024 + h * 128:tile * 1024 + (h + 1) * 128],
                   bank2[:, qt * 128:(qt + 1) * 128],
                   s_gs[:, 16 + (qid0 + qt) % 8:17 + (qid0 + qt) % 8],
                   None, ALU.mult,
                   wait=wq, inc=("tV", 1) if qt == 3 else None)
            bank_set(bank2, "tV", at("tV"))

        def drive_band(band, h0):
            units = []
            for hh in range(2):
                h = h0 + hh
                for g in range(4):
                    for qj in range(4):
                        qg = g * 4 + qj
                        qb = hh * 16 + qg
                        for kh in range(2):
                            units.append((hh, h, g, qj, qb, qg, kh))
            n = len(units)
            for ui in range(n + 5):
                if 0 <= ui - 2 < n:
                    qb2, kh2 = units[ui - 2][4], units[ui - 2][6]
                    u_exps(qb2, kh2, ST[qb2]["tnb%d" % kh2])
                if ui % 2 == 0:
                    if pend_ot[0] is not None:
                        emit_group(pend_ot[0])
                        pend_ot[0] = None
                    elif pend_group[0] is not None:
                        emit_group_av(pend_group[0])
                        pend_ot[0] = pend_group[0]
                        pend_group[0] = None
                if 0 <= ui - 4 < n:
                    qb4, kh4 = units[ui - 4][4], units[ui - 4][6]
                    if kh4 == 0:
                        u_sigma_d0(qb4)
                    u_transposes(qb4, kh4)
                    if kh4 == 1:
                        stp = ST[qb4]
                        grp_rs.append(stp["trs"])
                        if stp["qj"] == 3:
                            assert pend_group[0] is None
                            pend_group[0] = dict(stp, trs=grp_rs[-4:])
                        del ST[qb4]
                if ui < n:
                    hh, h, g, qj, qb, qg, kh = units[ui]
                    ST.setdefault(qb, {"qb": qb, "hh": hh, "h": h, "g": g,
                                       "qj": qj})
                    tmm = u_scores(band, hh, qb, qg, kh)
                    ST[qb]["tnb%d" % kh] = u_maxes(qb, kh, tmm)
                if ui % 2 == 1:
                    p4a_step()
            if pend_group[0] is not None:
                emit_group_av(pend_group[0])
                pend_ot[0] = pend_group[0]
                pend_group[0] = None
            if pend_ot[0] is not None:
                emit_group(pend_ot[0])
                pend_ot[0] = None

        rope_v_of = {}
        for band in range(4):
            barrier()
            h0 = 2 * band
            bcol = band * 512
            ev_qk = {id(s_qb): [0] * 8, id(s_kb): [0] * 8}

            def emit_vt(vt):
                bank = ps_c if vt % 2 == 0 else ps_d
                w = bank_wait(bank)
                for ec in range(8):
                    op("tensor", "matmul", bank[:, 0:512],
                       s_at[:, ec * 2048 + bcol + vt * 128:
                            ec * 2048 + bcol + (vt + 1) * 128],
                       s_wt3[:, ec * 1024: ec * 1024 + 512],
                       start=(ec == 0), stop=(ec == 7),
                       wait=w if ec == 0 else ())
                    w = []
                    op("tensor", "matmul", bank[:, 512:1024],
                       s_at[:, ec * 2048 + bcol + vt * 128:
                            ec * 2048 + bcol + (vt + 1) * 128],
                       s_wt3[:, ec * 1024 + 512: (ec + 1) * 1024],
                       start=(ec == 0), stop=(ec == 7),
                       inc=("tT", 1) if ec == 7 else None)
                tv = op("scalar", "copy", s_vb[:, vt * 1024:(vt + 1) * 1024],
                        bank, wait=[("tT", at("tT"))], inc=("tS", 1))
                bank_set(bank, "tS", tv)

            rope_pend = []

            def emit_rope(dstb, cs):
                d3r = dstb.rearrange("p (h c2 x) -> p h c2 x", h=2, c2=8)
                for c in cs:
                    bankr = (ps_g[:, 0:512] if c % 2 == 0
                             else ps_g[:, 512:1024])
                    w = bank_wait(ps_g, [("tS", ev_qk[id(dstb)][c])])
                    tmm = op("tensor", "matmul", bankr, s_pt, d3r[:, :, c, :],
                             start=True, stop=True, wait=w, inc=("tT", 1))
                    tabc = s_cos[:, c * 256:(c + 1) * 256]
                    tabs = s_sin[:, c * 256:(c + 1) * 256]
                    for hh in range(2):
                        ss = dstb[:, hh * 2048 + c * 256:
                                  hh * 2048 + (c + 1) * 256]
                        rr = bankr[:, hh * 256:(hh + 1) * 256]
                        op("vector", "tensor_tensor", ss, ss, tabc, ALU.mult,
                           wait=[("tT", tmm)] if hh == 0 else (),
                           inc=("tV", 1))
                        tv2 = op("vector", "tensor_tensor", rr, rr, tabs,
                                 ALU.mult, inc=("tV", 1))
                        op("vector", "tensor_tensor", ss, ss, rr, ALU.add,
                           wait=[("tV", tv2)], inc=("tV", 1))
                    bank_set(ps_g, "tV", at("tV"))

            qkp = 0
            for (wt, dstb) in ((s_wt1, s_qb), (s_wt2, s_kb)):
                for cp in range(4):
                    cs = (2 * cp, 2 * cp + 1)
                    ws = [bank_wait(ps_a), bank_wait(ps_b)]
                    tmm2 = [0, 0]
                    for ec in range(8):
                        for j in range(2):
                            c = cs[j]
                            bank = ps_a if j == 0 else ps_b
                            tmm2[j] = op(
                                "tensor", "matmul", bank,
                                wt[:, ec * 1024 + c * 128:
                                   ec * 1024 + (c + 1) * 128],
                                s_at[:, ec * 2048 + bcol:
                                     ec * 2048 + bcol + 512],
                                start=(ec == 0), stop=(ec == 7),
                                wait=ws[j] if ec == 0 else (),
                                inc=("tT", 1) if ec == 7 else None)
                            ws[j] = []
                    for j in range(2):
                        c = cs[j]
                        bank = ps_a if j == 0 else ps_b
                        d3 = dstb.rearrange("p (h c2 x) -> p h c2 x",
                                            h=2, c2=8)
                        tv = op("scalar", "copy", d3[:, :, c, :],
                                bank.rearrange("p (h x) -> p h x", h=2),
                                wait=[("tT", tmm2[j])], inc=("tS", 1))
                        ev_qk[id(dstb)][c] = tv
                        bank_set(bank, "tS", tv)
                    rope_pend.append((dstb, cs))
                    if len(rope_pend) > 1:
                        emit_rope(*rope_pend.pop(0))
                    if qkp % 2 == 1:
                        emit_vt(qkp // 2)
                    qkp += 1
            while rope_pend:
                emit_rope(*rope_pend.pop(0))
            t_b1_s = at("tS")
            if "vband" in tap and band == 0:
                op("gpsimd", "dma_start", out=tap["vband"], in_=s_vb,
                   wait=[("tS", t_b1_s)], inc=("dG", 16))
            rope_v = at("tV")
            if "qband" in tap and band == 0:
                op("gpsimd", "dma_start", out=tap["qband"], in_=s_qb,
                   wait=[("tV", rope_v)], inc=("dG", 16))
                op("gpsimd", "dma_start", out=tap["kband"], in_=s_kb,
                   wait=[("tV", rope_v)], inc=("dG", 16))
            rope_v_of[band] = rope_v
            drive_band(band, h0)
        if "attn" in tap:
            op("gpsimd", "dma_start", out=tap["attn"], in_=s_attn,
               wait=[("tS", at("tS"))], inc=("dG", 16))

    def gamma_pass_gen(d_w, rows, cols, gcol, out):
        npieces = (rows // 128) * (cols // 1024)
        per_row = cols // 1024
        assert npieces <= 32
        src3 = d_w.rearrange("(t p) (q e) -> t q p e", p=128, e=1024)
        red = [0] * npieces
        gbufs = (s_wfs4b[:, 0:1024], s_wfs4b[:, 1024:2048],
                 s_wfs.bitcast(F32)[:, 0:1024], s_wfs.bitcast(F32)[:, 1024:2048])
        gsems = ("dS", "dA", "dB", "dC")
        for t in range(npieces):
            buf = gbufs[t % 4]
            qeng = ("sync", "scalar")[t % 2]
            qsem = gsems[t % 4]
            w = [("tV", red[t - 4])] if t >= 4 else ()
            tl = op(qeng, "dma_start", out=buf,
                    in_=src3[t // per_row, t % per_row], inc=(qsem, 16),
                    wait=w)
            red[t] = op("vector", "reduce_sum", s_gs[:, t:t + 1], buf,
                        axis=AX.X, apply_absolute_value=True,
                        wait=[(qsem, tl)], inc=("tV", 1))
            yield
        tg = op("vector", "reduce_sum", s_nm[:, 22:23], s_gs[:, 0:npieces],
                axis=AX.X, wait=[("tV", red[npieces - 1])], inc=("tV", 1))
        w = bank_wait(ps_f, [("tV", tg)])
        tmm = op("tensor", "matmul", ps_f[0:1, 0:1], s_ones_col_f32,
                 s_nm[:, 22:23], start=True, stop=True, wait=w, inc=("tT", 1))
        tg = op("vector", "tensor_scalar", s_nm[0:1, 20:21], ps_f[0:1, 0:1],
                0.5 / (rows * cols), 0.5 * EPS_Q, ALU.mult, op1=ALU.add,
                wait=[("tT", tmm)], inc=("tV", 1))
        bank_set(ps_f, "tV", tg)
        w = bank_wait(ps_f, [("tV", tg)])
        tmm = op("tensor", "matmul", ps_f[:, 1:2], s_ones_row_f32,
                 s_nm[0:1, 20:21], start=True, stop=True, wait=w,
                 inc=("tT", 1))
        ts = op("scalar", "copy", s_tau[:, gcol:gcol + 1], ps_f[:, 1:2],
                wait=[("tT", tmm)], inc=("tS", 1))
        bank_set(ps_f, "tS", ts)
        out[0] = op("vector", "tensor_scalar", s_tau[:, gcol + 1:gcol + 2],
                    s_tau[:, gcol:gcol + 1], -1.0, None, ALU.mult,
                    wait=[("tS", ts)], inc=("tV", 1))
        yield

    q_chain = [0]

    def quantize_into(dst, src, gcol, wv):
        wv = list(wv)
        if q_chain[0]:
            wv.append(("tV", q_chain[0]))
        tq = op("vector", "tensor_single_scalar", s_abq[:, 0:src.shape[1]],
                src, s_tau[:, gcol + 1:gcol + 2], ALU.is_lt, wait=wv,
                inc=("tV", 1))
        q_chain[0] = op("vector", "scalar_tensor_tensor", dst, src,
                        s_tau[:, gcol:gcol + 1], s_abq[:, 0:src.shape[1]],
                        ALU.is_gt, ALU.subtract, wait=[("tV", tq)],
                        inc=("tV", 1))
        return q_chain[0]

    def p4b_gen():
        tau_out = [0]
        for _ in gamma_pass_gen(dw["wf2"], E, FF, 2, tau_out):
            yield
        t_tau = tau_out[0]
        prev_q = [0, 0, 0, 0]
        qbufs = (s_wfs4b[:, 0:1024], s_wfs4b[:, 1024:2048],
                 s_wfs.bitcast(F32)[:, 0:1024],
                 s_wfs.bitcast(F32)[:, 1024:2048])
        qsems4 = ("dS", "dA", "dB", "dC")
        tp_hist = [0, 0]
        for pair in range(4):
            q2sb = s_q2s2[pair % 2]
            for i in range(2):
                et = pair * 2 + i
                for qq in range(4):
                    n4 = qq
                    buf = qbufs[n4]
                    qeng = ("sync", "scalar")[qq % 2]
                    qsem = qsems4[n4]
                    w = []
                    if pair > 0 or i > 0:
                        w.append(("tV", prev_q[n4]))
                    else:
                        w.append(("tV", t_tau))
                    tl = op(qeng, "dma_start", out=buf,
                            in_=dw["wf2"][et * 128:(et + 1) * 128,
                                          qq * 1024:(qq + 1) * 1024],
                            inc=(qsem, 16), wait=w)
                    wv = [(qsem, tl)]
                    if pair == 0 and i == 0 and qq == 0:
                        wv.append(("tV", t_tau))
                    if i == 0 and qq < 2 and tp_hist[pair % 2]:
                        wv.append(("tT", tp_hist[pair % 2]))
                    prev_q[n4] = quantize_into(
                        q2sb[:, i * 4096 + qq * 1024:
                             i * 4096 + (qq + 1) * 1024], buf, 2, wv)
                    yield
            tq = at("tV")
            for fcg in range(4):
                for fc in range(fcg * 8, fcg * 8 + 8):
                    bank = ps_a if fc % 2 == 0 else ps_b
                    w = bank_wait(bank, [("tV", tq)])
                    for i in range(2):
                        op("tensor", "matmul",
                           bank[:, i * 128:(i + 1) * 128],
                           q2sb[:, i * 4096 + fc * 128:
                                i * 4096 + (fc + 1) * 128],
                           s_iden, start=True, stop=True,
                           wait=w if i == 0 else (),
                           inc=("tT", 1) if i == 1 else None)
                        w = []
                    tv = op("scalar", "copy",
                            s_q2t[:, fc * 1024 + pair * 256:
                                  fc * 1024 + (pair + 1) * 256],
                            bank[:, 0:256], wait=[("tT", at("tT"))],
                            inc=("tS", 1))
                    bank_set(bank, "tS", tv)
                yield
            tp_hist[pair % 2] = at("tT")
        while True:
            yield

    p4b_iter = [None]

    def p4b_step():
        if p4b_iter[0] is not None:
            next(p4b_iter[0])

    if phases >= 3:
        barrier()
        prev_stats_tV = 0
        prev_add2 = [0, 0]
        prev_st2 = [0, 0]
        prev_ts3 = 0
        t_embk = [0] * 16
        for k in range(16):
            u, rb = k // 8, k % 8
            bank = ps_c if k % 2 == 0 else ps_d
            w = bank_wait(bank)
            for c in range(8):
                lhs = s_wtop[:, c * 1024 + rb * 128: c * 1024 + (rb + 1) * 128]
                rhs = s_attn[:, (2 * c + u) * 1024:(2 * c + u + 1) * 1024]
                op("tensor", "matmul", bank[:, 0:512], lhs, rhs[:, 0:512],
                   start=(c == 0), stop=(c == 7), wait=w if c == 0 else ())
                w = []
                op("tensor", "matmul", bank[:, 512:1024], lhs,
                   rhs[:, 512:1024], start=(c == 0), stop=(c == 7),
                   inc=("tT", 1) if c == 7 else None)
            wy = [("tT", at("tT"))]
            if prev_stats_tV:
                wy.append(("tV", prev_stats_tV))
            if prev_ts3:
                wy.append(("tS", prev_ts3))
            tev = op("scalar", "copy", s_y3, bank, wait=wy, inc=("tS", 1))
            bank_set(bank, "tS", tev)
            row = k * 128
            eng3, sem3 = ("sync", "dB") if k % 2 == 0 else ("scalar", "dC")
            we = [("tV", prev_add2[k % 2])] if prev_add2[k % 2] else []
            t_embk[k] = op(eng3, "dma_start", out=s_emb3d[k % 2],
                           in_=d_emb[row:row + 128, :], wait=we,
                           inc=(sem3, 16))
            op("vector", "bn_stats", s_stats[:, 0:6], s_y3[:, 0:512],
               wait=[("tS", tev)])
            prev_stats_tV = op("vector", "bn_stats", s_stats[:, 6:12],
                               s_y3[:, 512:1024], inc=("tV", 1))
            tg = op("vector", "bn_aggr", s_bn4[:, 0:2], s_stats[:, 0:12],
                    wait=[("tV", prev_stats_tV)], inc=("tV", 1))
            tg = op("vector", "tensor_scalar", s_rstd4[:, 0:1], s_bn4[:, 1:2],
                    EPS_LN, None, ALU.add, wait=[("tV", tg)], inc=("tV", 1))
            ts1 = op("scalar", "activation", s_rstd4[:, 0:1], s_rstd4[:, 0:1],
                     AF.Ln, wait=[("tV", tg)], inc=("tS", 1))
            ts2 = op("scalar", "activation", s_rstd4[:, 0:1], s_rstd4[:, 0:1],
                     AF.Exp, scale=-0.5, wait=[("tS", ts1)], inc=("tS", 1))
            tg = op("vector", "tensor_tensor", s_mean4[:, 0:1], s_bn4[:, 0:1],
                    s_rstd4[:, 0:1], ALU.mult, wait=[("tS", ts2)],
                    inc=("tV", 1))
            tg = op("vector", "tensor_scalar", s_mean4[:, 0:1],
                    s_mean4[:, 0:1], -1.0, None, ALU.mult,
                    wait=[("tV", tg)], inc=("tV", 1))
            ts3 = op("scalar", "activation", s_n3, s_y3, AF.Identity,
                     bias=s_mean4[:, 0:1], scale=s_rstd4[:, 0:1],
                     wait=[("tV", tg)], inc=("tS", 1))
            prev_ts3 = ts3
            wadd = [("tS", ts3), (sem3, t_embk[k])]
            if prev_st2[k % 2]:
                wadd.append((("dG", "dV")[k % 2], prev_st2[k % 2]))
            prev_add2[k % 2] = op("vector", "tensor_tensor", s_x1t3d[k % 2],
                                  s_emb3d[k % 2], s_n3, ALU.add, wait=wadd,
                                  inc=("tV", 1))
            prev_st2[k % 2] = op("gpsimd", "dma_start",
                                 out=d_x1[row:row + 128, :],
                                 in_=s_x1t3d[k % 2],
                                 wait=[("tV", prev_add2[k % 2])],
                                 inc=(("dG", "dV")[k % 2], 16))
            for _ in range(3):
                p4b_step()
        if "x1" in tap:
            op("gpsimd", "dma_start", out=tap["x1"], in_=d_x1,
               wait=[("dG", at("dG"))], inc=("dG", 16))

    if phases >= 4:
        for _ in range(120):
            p4a_step()
        if "q1tt" in tap:
            op("gpsimd", "dma_start", out=tap["q1tt"], in_=d_q1t,
               wait=[("dG", at("dG"))], inc=("dG", 16))
        if "tau" in tap:
            op("gpsimd", "dma_start", out=tap["tau"][:, 0:2], in_=s_tau[:, 0:2],
               wait=[("tV", at("tV"))], inc=("dG", 16))

        barrier()
        p4b_iter[0] = p4b_gen()
        for _ in range(130):
            p4b_step()
        if "q2t" in tap:
            op("gpsimd", "dma_start", out=tap["q2t"], in_=s_q2t,
               wait=[("tS", at("tS"))], inc=("dG", 16))

    if phases >= 5:
        barrier()
        t_b2l = op("sync", "dma_start", out=s_b2row, in_=dw["b2"][None, :],
                   inc=("dS", 16))
        w = bank_wait(ps_c, [("dS", t_b2l)])
        for half in range(2):
            op("tensor", "matmul", ps_c[:, half * 512:(half + 1) * 512],
               s_ones_row_f32, s_b2row[:, half * 512:(half + 1) * 512],
               start=True, stop=True, wait=w if half == 0 else (),
               inc=("tT", 1) if half == 1 else None)
            w = []
        t_b2 = op("scalar", "copy", s_b2b, ps_c, wait=[("tT", at("tT"))],
                  inc=("tS", 1))
        bank_set(ps_c, "tS", t_b2)

        prev_hev = [0] * 32
        prev_out_guard = [0, 0]
        prev_dm5 = [0]
        prev_cast5 = [0]
        tx1_hist = [None, None]
        prev_x1sem = [0, 0]
        q1t_tq = [0] * 32
        q1rez_tq = [0] * 32
        q1sems = ("dS", "dA", "dB", "dC")
        tout_hist = [[0] * 4, [0] * 4]
        t_ffn2mm = [0, 0]
        def emit_x1_in(ch):
            if ch >= 4:
                return
            x1b = s_x1q2[ch % 2]
            tx1 = []
            for i in range(4):
                row = (ch * 4 + i) * 128
                w = []
                if tout_hist[ch % 2][i]:
                    w.append(("tV", tout_hist[ch % 2][i]))
                if prev_x1sem[i % 2]:
                    w.append((("dX", "dY")[i % 2], prev_x1sem[i % 2]))
                prev_x1sem[i % 2] = op(("sync", "scalar")[i % 2], "dma_start",
                                       out=x1b[:, i * 1024:(i + 1) * 1024],
                                       in_=d_x1[row:row + 128, :],
                                       inc=(("dX", "dY")[i % 2], 16), wait=w)
                tx1.append(prev_x1sem[i % 2])
            tx1_hist[ch % 2] = tx1

        barrier()
        emit_x1_in(0)
        for ch in range(4):
            x1b = s_x1q2[ch % 2]
            a2b = s_a2t5x[ch % 2]
            tx1 = tx1_hist[ch % 2]
            prev_dm_c = prev_dm5[0]
            prev_cast_c = prev_cast5[0]
            for g2 in range(2):
                x0 = x1b[:, (g2 * 2) * 1024:(g2 * 2 + 1) * 1024]
                x1 = x1b[:, (g2 * 2 + 1) * 1024:(g2 * 2 + 2) * 1024]
                op("vector", None, wait=[("dX", tx1[2]), ("dY", tx1[3])])
                t_rstd = ln_stats_pair(x0, x1, prev_cast_c)
                prev_cast_c, prev_dm_c = ln_transpose_pair(
                    x0, x1, s_xc5, g2 * 256, t_rstd, a2b, 512, prev_dm_c)
            prev_dm5[0] = prev_dm_c
            prev_cast5[0] = prev_cast_c
            t_a2 = at("tS")
            if "a2t" in tap and ch == 0:
                op("gpsimd", "dma_start", out=tap["a2t"], in_=a2b,
                   wait=[("tS", t_a2)], inc=("dG", 16))
            for fp in range(16):
                fbs = (2 * fp, 2 * fp + 1)
                tqs = []
                for fb in fbs:
                    qbuf = s_q1rez[:, fb * 1024:(fb + 1) * 1024]
                    qsem = q1sems[fb % 4]
                    if ch > 0:
                        tqs.append((qsem, q1rez_tq[fb]))
                        continue
                    qeng = ("sync", "scalar")[fb % 2]
                    w = [("tS", prev_hev[fb - 4])] if fb >= 4 else []
                    tq = op(qeng, "dma_start", out=qbuf, in_=d_q1t[fb],
                            inc=(qsem, 16), wait=w)
                    q1rez_tq[fb] = tq
                    tqs.append((qsem, tq))
                ws = [bank_wait(ps_a, [tqs[0], ("tS", t_a2)]),
                      bank_wait(ps_b, [tqs[1], ("tS", t_a2)])]
                tmm2 = [0, 0]
                for ec in range(8):
                    for j in range(2):
                        fb = fbs[j]
                        bank = ps_a if j == 0 else ps_b
                        qb5 = s_q1rez[:, fb * 1024:(fb + 1) * 1024]
                        tmm2[j] = op("tensor", "matmul", bank,
                                     qb5[:, ec * 128:(ec + 1) * 128],
                                     a2b[:, ec * 512:(ec + 1) * 512],
                                     start=(ec == 0), stop=(ec == 7),
                                     wait=ws[j] if ec == 0 else (),
                                     inc=("tT", 1) if ec == 7 else None)
                        ws[j] = []
                for j in range(2):
                    fb = fbs[j]
                    bank = ps_a if j == 0 else ps_b
                    wev = [("tT", tmm2[j])]
                    if fb == 0 and t_ffn2mm[0]:
                        wev.append(("tT", t_ffn2mm[0]))
                    prev_hev[fb] = op("scalar", "activation",
                                      s_ht[:, fb * 512:(fb + 1) * 512], bank,
                                      AF.Relu, bias=s_b1t[:, fb:fb + 1],
                                      wait=wev, inc=("tS", 1))
                    bank_set(bank, "tS", prev_hev[fb])
            emit_x1_in(ch + 1)
            t_h = at("tS")
            if "ht" in tap and ch == 0:
                op("gpsimd", "dma_start", out=tap["ht"], in_=s_ht,
                   wait=[("tS", t_h)], inc=("dG", 16))
            for sb in range(4):
                bank = ps_c if sb % 2 == 0 else ps_d
                fbuf = s_f5 if sb % 2 == 0 else s_f5b
                obuf = s_o5 if sb % 2 == 0 else s_o5b
                w = bank_wait(bank, [("tS", t_h)])
                for fc in range(32):
                    lhs = s_ht[:, fc * 512 + sb * 128:
                               fc * 512 + (sb + 1) * 128]
                    op("tensor", "matmul", bank[:, 0:512], lhs,
                       s_q2t[:, fc * 1024: fc * 1024 + 512],
                       start=(fc == 0), stop=(fc == 31),
                       wait=w if fc == 0 else ())
                    w = []
                    op("tensor", "matmul", bank[:, 512:1024], lhs,
                       s_q2t[:, fc * 1024 + 512:(fc + 1) * 1024],
                       start=(fc == 0), stop=(fc == 31),
                       inc=("tT", 1) if fc == 31 else None)
                wf = [("tT", at("tT")), ("tS", t_b2)]
                if prev_out_guard[sb % 2]:
                    wf.append(("tV", prev_out_guard[sb % 2]))
                tfv = op("vector", "tensor_tensor", fbuf, bank, s_b2b,
                         ALU.add, wait=wf, inc=("tV", 1))
                bank_set(bank, "tV", tfv)
                op("vector", "bn_stats", s_stats[:, 24:30], fbuf[:, 0:512],
                   wait=[("tV", tfv)])
                tg = op("vector", "bn_stats", s_stats[:, 30:36],
                        fbuf[:, 512:1024], inc=("tV", 1))
                tg = op("vector", "bn_aggr", s_bn4[:, 4:6], s_stats[:, 24:36],
                        wait=[("tV", tg)], inc=("tV", 1))
                tg = op("vector", "tensor_scalar", s_rstd4[:, 2:3],
                        s_bn4[:, 5:6], EPS_LN, None, ALU.add,
                        wait=[("tV", tg)], inc=("tV", 1))
                ts1 = op("scalar", "activation", s_rstd4[:, 2:3],
                         s_rstd4[:, 2:3], AF.Ln, wait=[("tV", tg)],
                         inc=("tS", 1))
                ts2 = op("scalar", "activation", s_rstd4[:, 2:3],
                         s_rstd4[:, 2:3], AF.Exp, scale=-0.5,
                         wait=[("tS", ts1)], inc=("tS", 1))
                tg = op("vector", "tensor_tensor", s_mean4[:, 2:3],
                        s_bn4[:, 4:5], s_rstd4[:, 2:3], ALU.mult,
                        wait=[("tS", ts2)], inc=("tV", 1))
                tg = op("vector", "tensor_scalar", s_mean4[:, 2:3],
                        s_mean4[:, 2:3], -1.0, None, ALU.mult,
                        wait=[("tV", tg)], inc=("tV", 1))
                ts3 = op("scalar", "activation", obuf, fbuf, AF.Identity,
                         bias=s_mean4[:, 2:3], scale=s_rstd4[:, 2:3],
                         wait=[("tV", tg)], inc=("tS", 1))
                tout = op("vector", "tensor_tensor", obuf,
                          x1b[:, sb * 1024:(sb + 1) * 1024], obuf, ALU.add,
                          wait=[("tS", ts3)], inc=("tV", 1))
                tout_hist[ch % 2][sb] = tout
                row = (ch * 4 + sb) * 128
                t_st = op("gpsimd", "dma_start", out=d_out[row:row + 128, :],
                          in_=obuf, wait=[("tV", tout)], inc=("dG", 16))
                prev_out_guard[sb % 2] = op("vector", "engine_nop",
                                            wait=[("dG", t_st)],
                                            inc=("tV", 1))
            t_ffn2mm[0] = at("tT")
    else:
        op("gpsimd", "dma_start", out=d_out[0:128, :],
           in_=big1.bitcast(F32)[:, 0:1024],
           wait=[("tS", at("tS")), ("tV", at("tV"))], inc=("dG", 16))

    barrier(("gpsimd",))
    pr.emit()
    return nc, pr.nops


def kernel(L0_attn_emb, L0_attn_W1, L0_attn_W2, L0_attn_W3, L0_attn_Wo,
           L0_ffn_W1, L0_ffn_b1, L0_ffn_W2, L0_ffn_b2):
    C = host_constants()
    nc, _ = build_nc(taps=TAPS, phases=PHASES)
    emb = np.ascontiguousarray(np.asarray(L0_attn_emb, dtype=np.float32))
    base = {
        "w1": np.ascontiguousarray(L0_attn_W1, dtype=np.float32),
        "w2": np.ascontiguousarray(L0_attn_W2, dtype=np.float32),
        "w3": np.ascontiguousarray(L0_attn_W3, dtype=np.float32),
        "wo": np.ascontiguousarray(L0_attn_Wo, dtype=np.float32),
        "wf1": np.ascontiguousarray(L0_ffn_W1, dtype=np.float32),
        "b1": np.ascontiguousarray(L0_ffn_b1, dtype=np.float32),
        "wf2": np.ascontiguousarray(L0_ffn_W2, dtype=np.float32),
        "b2": np.ascontiguousarray(L0_ffn_b2, dtype=np.float32),
    }
    base.update(C)
    in_maps = [dict(base, emb=emb[b]) for b in range(8)]
    res = run_bass_kernel_spmd(nc, in_maps, core_ids=list(range(8)))
    return np.stack([np.asarray(r["out"]) for r in res.results])


# version 54
# speedup vs baseline: 1.1186x; 1.0117x over previous
import math
import numpy as np
import ml_dtypes
import concourse.bass as bass
import concourse.mybir as mybir
from concourse.bass_utils import run_bass_kernel_spmd

F32 = mybir.dt.float32
BF16 = mybir.dt.bfloat16
F8 = mybir.dt.float8e4
AF = mybir.ActivationFunctionType
ALU = mybir.AluOpType
AX = mybir.AxisListType
BF = ml_dtypes.bfloat16

S, E, P, H, D, FF = 2048, 1024, 1024, 8, 128, 4096
EPS_LN = 1e-5
EPS_Q = 1e-5
ISQRT8 = float(1.0 / math.sqrt(8.0))

TAPS = ()
PHASES = 99


def host_constants():
    iden = np.eye(128, dtype=BF)
    Prot = np.zeros((128, 128), np.float32)
    for i in range(64):
        Prot[2 * i, 2 * i + 1] = -1.0
        Prot[2 * i + 1, 2 * i] = 1.0
    invf2 = np.repeat(1.0 / 10000.0 ** (np.arange(0, D, 2) / D), 2)
    u = np.arange(256, dtype=np.float64)
    cosT = np.empty((128, 2048), np.float64)
    sinT = np.empty((128, 2048), np.float64)
    for c in range(8):
        ang = (u * 8 + c)[None, :] * invf2[:, None]
        cosT[:, c * 256:(c + 1) * 256] = np.cos(ang)
        sinT[:, c * 256:(c + 1) * 256] = np.sin(ang)
    return {"c_iden": iden,
            "c_ones_row_bf": np.ones((1, 128), dtype=BF),
            "c_ones_row_f32": np.ones((1, 128), dtype=np.float32),
            "c_ones_col_bf": np.ones((128, 1), dtype=BF),
            "c_ones_col_f32": np.ones((128, 1), dtype=np.float32),
            "c_pt": np.ascontiguousarray(Prot.T).astype(BF),
            "c_iden_f32": np.eye(128, dtype=np.float32),
            "c_cos": cosT.astype(BF), "c_sin": sinT.astype(BF)}


class Prog:
    ENGINES = ("sync", "scalar", "vector", "tensor", "gpsimd")
    SEMS = ("dS", "dG", "dA", "tT", "tV", "tS", "tG", "dV",
            "dB", "dC", "dX", "dY", "dZ")

    def __init__(self, nc):
        self.nc = nc
        self.plan = {e: [] for e in self.ENGINES}
        self.sem = {}
        self.cnt = {}
        self.nops = 0
        for s in self.SEMS:
            self.sem[s] = nc.semaphore(s).__enter__()
            self.cnt[s] = 0

    def op(self, eng, meth, *args, wait=(), inc=None, **kw):
        self.plan[eng].append((meth, args, kw, tuple(wait), inc))
        self.nops += 1
        if inc is not None:
            self.cnt[inc[0]] += inc[1]
            return self.cnt[inc[0]]
        return None

    def at(self, sem):
        return self.cnt[sem]

    def barrier(self, engs=None):
        ws = tuple((s, self.cnt[s]) for s in self.SEMS if self.cnt[s] > 0)
        for e in (engs or self.ENGINES):
            self.plan[e].append((None, (), {}, ws, None))

    def emit(self):
        blk = self.nc.Block().__enter__()
        for eng in self.ENGINES:
            items = self.plan[eng]
            if not items:
                continue

            def body(e, items=items):
                seen = {}
                for meth, args, kw, waits, inc in items:
                    for sname, v in waits:
                        if v <= seen.get(sname, 0):
                            continue
                        e.wait_ge(self.sem[sname], v)
                        seen[sname] = v
                    if meth is None:
                        continue
                    ins = getattr(e, meth)(*args, **kw)
                    if inc is not None:
                        ins.then_inc(self.sem[inc[0]], inc[1])
            getattr(blk, eng)(body)
        blk.__exit__(None, None, None)


def build_nc(taps=(), phases=99):
    nc = bass.Bass("TRN2", target_bir_lowering=False, debug=False)
    pr = Prog(nc)
    op, at, barrier = pr.op, pr.at, pr.barrier
    C = host_constants()

    d_emb = nc.dram_tensor("emb", [S, E], F32, kind="ExternalInput").ap()
    dw = {n: nc.dram_tensor(n, shp, F32, kind="ExternalInput").ap()
          for n, shp in (("w1", [P, E]), ("w2", [P, E]), ("w3", [P, E]),
                         ("wo", [E, P]), ("wf1", [FF, E]), ("b1", [FF]),
                         ("wf2", [E, FF]), ("b2", [E]))}
    d_out = nc.dram_tensor("out", [S, E], F32, kind="ExternalOutput").ap()
    dc = {k: nc.dram_tensor(k, list(v.shape),
                            BF16 if v.dtype == BF else F32,
                            kind="ExternalInput").ap() for k, v in C.items()}
    d_x1 = nc.dram_tensor("x1s", [S, E], F32, kind="Internal").ap()
    d_q1t = nc.dram_tensor("q1ts", [32, 128, 1024], F8, kind="Internal").ap()

    TS = {"wt1": ([128, 8192], BF16), "wtop": ([128, 8192], BF16),
          "at": ([128, 16384], BF16), "qband": ([128, 4096], BF16),
          "kband": ([128, 4096], BF16), "vband": ([128, 4096], BF16),
          "eb": ([128, 8192], BF16), "attn": ([128, 16384], BF16),
          "x1": ([S, E], F32), "a2t": ([128, 4096], BF16),
          "ht": ([128, 16384], BF16), "q2t": ([128, 32768], BF16),
          "q1tt": ([32, 8, 128, 128], BF16), "mhat": ([128, 32], F32),
          "sig": ([128, 2048], F32), "otn": ([128, 2048], BF16),
          "tau": ([128, 8], F32), "xcdbg": ([128, 2048], BF16),
          "dmdbg": ([128, 256], BF16), "scdbg": ([128, 8], F32)}
    tap = {t: nc.dram_tensor("tap_" + t, list(TS[t][0]), TS[t][1],
                             kind="ExternalOutput").ap() for t in taps}

    A = {}

    def sb(name, shape, dtype):
        apx = nc.alloc_sbuf_tensor(name, shape, dtype).ap()
        A[name] = apx
        return apx

    big1 = sb("big1", [128, 32768], BF16)
    s_at = big1[:, 0:16384]
    s_attn = big1[:, 16384:32768]
    s_q2t = big1.bitcast(F8)[:, 0:32768]
    s_q1rez = big1.bitcast(F8)[:, 32768:65536]

    big2 = sb("big2", [128, 24576], BF16)
    s_wt1 = big2[:, 0:8192]
    s_wt2 = big2[:, 8192:16384]
    s_wt3 = big2[:, 16384:24576]
    s_ht = big2[:, 0:16384]
    s_x1q = big2[:, 16384:24576].bitcast(F32)

    big3 = sb("big3", [128, 8192], BF16)
    s_wtop = big3
    s_q1d = (big3[:, 0:1024], big3[:, 1024:2048])
    s_a2t5 = big3[:, 2048:6144]
    s_o5b = big3[:, 6144:8192].bitcast(F32)

    big4 = sb("big4", [128, 8192], BF16)
    s_wnat = big4
    s_et = big4
    s_f5 = big4[:, 0:2048].bitcast(F32)
    s_f5b = big4[:, 2048:4096].bitcast(F32)
    s_b2b = big4[:, 4096:6144].bitcast(F32)
    s_o5 = big4[:, 6144:8192].bitcast(F32)

    big5 = sb("big5", [128, 4096], BF16)
    s_xt = big5.bitcast(F32)
    s_sq = big5
    s_y3 = big5[:, 0:2048].bitcast(F32)
    s_n3 = big5[:, 2048:4096].bitcast(F32)
    s_wfs = big5.bitcast(F32)

    big6 = sb("big6", [128, 4096], BF16)
    s_xc = big6[:, 0:2048]
    s_abq = big6.bitcast(F32)[:, 0:1024]

    s_xc5 = big6[:, 0:2048]

    big7 = sb("big7", [128, 12288], BF16)
    s_qb = big7[:, 0:4096]
    s_kb = big7[:, 4096:8192]
    s_vb = big7[:, 8192:12288]
    s_q2s = big7[:, 0:8192]
    s_q2s2 = (big7[:, 0:8192], big2[:, 0:8192])
    s_wfs4b = big7.bitcast(F32)[:, 4096:6144]
    s_q1stage = big7[:, 8192:10240]

    big8 = sb("big8", [128, 6144], BF16)
    s_sg = big8[:, 0:2048]
    s_otn = big8[:, 2048:4096]
    s_rsigb = big8[:, 4096:5120].bitcast(F32)
    s_rsig = big8[0:1, 5120:6144].bitcast(F32)
    s_x1t3 = big8[:, 4096:6144].bitcast(F32)
    s_q1d4 = (big5.bitcast(F8)[:, 0:1024], big5.bitcast(F8)[:, 1024:2048],
              big5.bitcast(F8)[:, 2048:3072],
              big5.bitcast(F8)[:, 3072:4096])
    s_x1q2 = (s_x1q, big7[:, 2048:10240].bitcast(F32))
    s_a2t5x = (s_a2t5, big8[:, 0:4096])
    s_emb3d = (big8.bitcast(F32)[:, 0:1024],
               big8.bitcast(F32)[:, 1024:2048])
    s_x1t3d = (s_x1t3, big6.bitcast(F32)[:, 1024:2048])

    s_iden = sb("s_iden", [128, 128], BF16)
    s_iden_f32 = sb("s_iden_f32", [128, 128], F32)
    s_b1s = sb("s_b1s", [32, 128], F32)
    s_pt = sb("s_pt", [128, 128], BF16)
    s_ones_row_bf = sb("s_ones_row_bf", [1, 128], BF16)
    s_ones_row_f32 = sb("s_ones_row_f32", [1, 128], F32)
    s_ones_col_bf = sb("s_ones_col_bf", [128, 1], BF16)
    s_ones_col_f32 = sb("s_ones_col_f32", [128, 1], F32)
    s_cos = sb("s_cos", [128, 2048], BF16)
    s_sin = sb("s_sin", [128, 2048], BF16)
    s_stats = sb("s_stats", [128, 40], F32)
    s_bn4 = sb("s_bn4", [128, 8], F32)
    s_mean4 = sb("s_mean4", [128, 4], F32)
    s_rstd4 = sb("s_rstd4", [128, 4], F32)
    s_dmat = sb("s_dmat", [128, 256], BF16)
    s_nm = sb("s_nm", [128, 32], F32)
    s_eb = sb("s_eb", [128, 1], F32)
    s_b1t = sb("s_b1t", [128, 32], F32)
    s_b2row = s_b2b[0:1, :]
    s_gs = sb("s_gs", [128, 32], F32)
    s_tau = sb("s_tau", [128, 4], F32)

    per_part = sum(a.tensor.shape[1] * (2 if a.dtype == BF16 else 4)
                   for a in A.values())
    assert per_part <= 212 * 1024, per_part

    ps_a = nc.alloc_psum_tensor("ps_a", [128, 512], F32).ap()
    ps_b = nc.alloc_psum_tensor("ps_b", [128, 512], F32).ap()
    ps_c = nc.alloc_psum_tensor("ps_c", [128, 1024], F32).ap()
    ps_d = nc.alloc_psum_tensor("ps_d", [128, 1024], F32).ap()
    ps_g = nc.alloc_psum_tensor("ps_g", [128, 1024], F32).ap()
    ps_e = ps_a
    ps_f = ps_b
    bg = {}

    def bank_wait(b, extra=()):
        w = list(extra)
        if id(b.tensor) in bg:
            s, v = bg[id(b.tensor)]
            w.append((s, v))
        return w

    def bank_set(b, sem, v):
        bg[id(b.tensor)] = (sem, v)

    for k, dst in (("c_iden", s_iden), ("c_pt", s_pt),
                   ("c_ones_row_bf", s_ones_row_bf),
                   ("c_ones_row_f32", s_ones_row_f32),
                   ("c_ones_col_bf", s_ones_col_bf),
                   ("c_ones_col_f32", s_ones_col_f32),
                   ("c_iden_f32", s_iden_f32),
                   ("c_cos", s_cos), ("c_sin", s_sin)):
        op("sync", "dma_start", out=dst, in_=dc[k], inc=("dS", 16))
    op("sync", "dma_start", out=s_b1s,
       in_=dw["b1"].rearrange("(t p) -> t p", p=128), inc=("dS", 16))
    t_consts = at("dS")
    tmm = op("tensor", "matmul", ps_f[:, 0:32], s_b1s, s_iden_f32[0:32, 0:32],
             start=True, stop=True, wait=[("dS", t_consts)], inc=("tT", 1))
    tb1 = op("scalar", "copy", s_b1t, ps_f[:, 0:32], wait=[("tT", tmm)],
             inc=("tS", 1))
    bank_set(ps_f, "tS", tb1)

    def transpose_groups(groups, wait_first):
        first = True
        for gi, (lhss, dst) in enumerate(groups):
            bank = ps_a if gi % 2 == 0 else ps_b
            w = bank_wait(bank, wait_first if first else ())
            first = False
            n = len(lhss)
            for q, lhs in enumerate(lhss):
                op("tensor", "matmul", bank[:, q * 128:(q + 1) * 128], lhs,
                   s_iden, start=True, stop=True, wait=w if q == 0 else (),
                   inc=("tT", 1) if q == n - 1 else None)
                w = []
            tmm = at("tT")
            tv = op("scalar", "copy", dst, bank[:, 0:128 * n],
                    wait=[("tT", tmm)], inc=("tS", 1))
            bank_set(bank, "tS", tv)

    s_wnat2 = big1[:, 16384:24576]

    def wnat_load(d_w, buf, sem, wv):
        src = d_w.rearrange("(t p) e -> p t e", p=128)
        return op("gpsimd", "dma_start",
                  out=buf.rearrange("p (t e) -> p t e", t=8), in_=src,
                  inc=(sem, 16), wait=wv)

    def wnat_transposes(buf, s_dst, wait_first, permute=False):
        groups = []
        for ec in range(8):
            for half in range(2):
                if permute:
                    lhss = [buf[:, rb * 1024 + ec: rb * 1024 + 1024: 8]
                            for rb in range(half * 4, half * 4 + 4)]
                else:
                    lhss = [buf[:, pb * 1024 + ec * 128:
                                pb * 1024 + (ec + 1) * 128]
                            for pb in range(half * 4, half * 4 + 4)]
                groups.append((lhss, s_dst[:, ec * 1024 + half * 512:
                                           ec * 1024 + (half + 1) * 512]))
        transpose_groups(groups, wait_first)
        return at("tT")

    if phases >= 0:
        tl1 = wnat_load(dw["w1"], s_wnat, "dG", [("dS", t_consts)])
        tl2 = wnat_load(dw["w2"], s_wnat2, "dZ", [])
        tf1 = wnat_transposes(s_wnat, s_wt1, [("dG", tl1)])
        tl3 = wnat_load(dw["w3"], s_wnat, "dG", [("tT", tf1)])
        tf2 = wnat_transposes(s_wnat2, s_wt2, [("dZ", tl2)])
        tl4 = wnat_load(dw["wo"], s_wnat2, "dZ", [("tT", tf2)])
        wnat_transposes(s_wnat, s_wt3, [("dG", tl3)])
        wnat_transposes(s_wnat2, s_wtop, [("dZ", tl4)], permute=True)
        if "wt1" in tap:
            op("gpsimd", "dma_start", out=tap["wt1"], in_=s_wt1,
               wait=[("tS", at("tS"))], inc=("dG", 16))
        if "wtop" in tap:
            op("gpsimd", "dma_start", out=tap["wtop"], in_=s_wtop,
               wait=[("tS", at("tS"))], inc=("dG", 16))

    def ln_stats_pair(x0, x1, prev_cast_tS):
        w = [("tS", prev_cast_tS)] if prev_cast_tS else []
        op("vector", "bn_stats", s_stats[:, 0:6], x0[:, 0:512], wait=w)
        op("vector", "bn_stats", s_stats[:, 6:12], x0[:, 512:1024])
        op("vector", "bn_stats", s_stats[:, 12:18], x1[:, 0:512])
        tg = op("vector", "bn_stats", s_stats[:, 18:24], x1[:, 512:1024],
                inc=("tV", 1))
        op("vector", "bn_aggr", s_bn4[:, 0:2], s_stats[:, 0:12],
           wait=[("tV", tg)])
        tg = op("vector", "bn_aggr", s_bn4[:, 2:4], s_stats[:, 12:24],
                inc=("tV", 1))
        op("vector", "tensor_scalar", s_mean4[:, 0:2], s_bn4[:, 0:4:2],
           -1.0, None, ALU.mult, wait=[("tV", tg)])
        tg = op("vector", "tensor_scalar", s_rstd4[:, 0:2], s_bn4[:, 1:4:2],
                EPS_LN, None, ALU.add, inc=("tV", 1))
        ts1 = op("scalar", "activation", s_rstd4[:, 0:2], s_rstd4[:, 0:2],
                 AF.Ln, wait=[("tV", tg)], inc=("tS", 1))
        ts2 = op("scalar", "activation", s_rstd4[:, 0:2], s_rstd4[:, 0:2],
                 AF.Exp, scale=-0.5, wait=[("tS", ts1)], inc=("tS", 1))
        return ts2

    def ln_transpose_pair(x0, x1, xc, dst_cols, t_rstd, dst, dst_stride,
                          prev_dmat_tT):
        tcast = 0
        for i, x in enumerate((x0, x1)):
            tcast = op("scalar", "activation", xc[:, i * 1024:(i + 1) * 1024],
                       x, AF.Identity, bias=s_mean4[:, i:i + 1],
                       wait=[("tS", t_rstd)] if i == 0 else (),
                       inc=("tS", 1))
        w = [("tS", t_rstd)]
        if prev_dmat_tT:
            w.append(("tT", prev_dmat_tT))
        td = 0
        for i in range(2):
            td = op("vector", "tensor_scalar",
                    s_dmat[:, i * 128:(i + 1) * 128],
                    s_iden, s_rstd4[:, i:i + 1], None, ALU.mult,
                    wait=w if i == 0 else (), inc=("tV", 1))
            w = []
        last = 0
        for ec in range(8):
            bank = ps_a if ec % 2 == 0 else ps_b
            w = bank_wait(bank, [("tS", tcast), ("tV", td)])
            for i in range(2):
                op("tensor", "matmul", bank[:, i * 128:(i + 1) * 128],
                   xc[:, i * 1024 + ec * 128: i * 1024 + (ec + 1) * 128],
                   s_dmat[:, i * 128:(i + 1) * 128], start=(i == 0),
                   stop=(i == 1),
                   wait=w if i == 0 else (),
                   inc=("tT", 1) if i == 1 else None)
                w = []
            tmm = at("tT")
            tv = op("scalar", "copy",
                    dst[:, ec * dst_stride + dst_cols: ec * dst_stride + dst_cols + 256],
                    bank[:, 0:256], wait=[("tT", tmm)], inc=("tS", 1))
            bank_set(bank, "tS", tv)
            last = tv
        return last, at("tT")

    if phases >= 1:
        xt4 = (big5.bitcast(F32)[:, 0:1024], big5.bitcast(F32)[:, 1024:2048],
               big8.bitcast(F32)[:, 0:1024], big8.bitcast(F32)[:, 1024:2048])
        xsems = ("dS", "dA", "dB", "dC")
        prev_cast2 = [0, 0]
        prev_dm = 0
        prev_cast = 0
        for g in range(8):
            p = g % 2
            tl = []
            for i in range(2):
                t = g * 2 + i
                n4 = p * 2 + i
                wld = ([("tS", prev_cast2[p])] if prev_cast2[p]
                       else [("dS", t_consts)])
                tl.append(op(("sync", "scalar")[i], "dma_start", out=xt4[n4],
                             in_=d_emb[t * 128:(t + 1) * 128, :],
                             wait=wld, inc=(xsems[n4], 16)))
            x0 = xt4[p * 2]
            x1 = xt4[p * 2 + 1]
            op("vector", None, wait=[(xsems[p * 2], tl[0]),
                                     (xsems[p * 2 + 1], tl[1])])
            t_rstd = ln_stats_pair(x0, x1, 0)
            prev_cast, prev_dm = ln_transpose_pair(
                x0, x1, s_xc, g * 256, t_rstd, s_at, 2048, prev_dm)
            prev_cast2[p] = prev_cast
            if g == 1 and "xcdbg" in tap:
                op("gpsimd", "dma_start", out=tap["xcdbg"], in_=s_xc,
                   wait=[("tS", prev_cast)], inc=("dG", 16))
                op("gpsimd", "dma_start", out=tap["dmdbg"], in_=s_dmat,
                   wait=[("tS", prev_cast)], inc=("dG", 16))
                op("gpsimd", "dma_start", out=tap["scdbg"][:, 0:2],
                   in_=s_mean4, wait=[("tS", prev_cast)], inc=("dG", 16))
                op("gpsimd", "dma_start", out=tap["scdbg"][:, 2:4],
                   in_=s_rstd4, wait=[("tS", prev_cast)], inc=("dG", 16))
                op("gpsimd", "dma_start", out=tap["scdbg"][:, 4:8],
                   in_=s_bn4, wait=[("tS", prev_cast)], inc=("dG", 16))
                barrier()
        t_p1_done = at("tS")
        if "at" in tap:
            op("gpsimd", "dma_start", out=tap["at"], in_=s_at,
               wait=[("tS", t_p1_done)], inc=("dG", 16))

    p4a_wfsA = big5.bitcast(F32)[:, 0:1024]
    p4a_wfsB = big5.bitcast(F32)[:, 1024:2048]
    p4a_stage = big5.bitcast(F8)[:, 4096:6144]
    p4a_A = big6[:, 0:1024]
    p4a_B = big6[:, 1024:2048]
    p4a_pair = big6[:, 2048:4096]

    def p4a_gen(first_wait):
        src_w1 = dw["wf1"].rearrange("(t p) e -> t p e", p=128)
        red = [0] * 32
        for t in range(32):
            buf = p4a_wfsA if t % 2 == 0 else p4a_wfsB
            qeng, qsem = ("sync", "dS") if t % 2 == 0 else ("scalar", "dA")
            w = [("tV", red[t - 2])] if t >= 2 else list(first_wait)
            tl = op(qeng, "dma_start", out=buf, in_=src_w1[t], inc=(qsem, 16),
                    wait=w)
            red[t] = op("vector", "reduce_sum", s_stats[:, t:t + 1], buf,
                        axis=AX.X, apply_absolute_value=True,
                        wait=[(qsem, tl)], inc=("tV", 1))
            yield
        tg = op("vector", "reduce_sum", s_stats[:, 32:33], s_stats[:, 0:32],
                axis=AX.X, wait=[("tV", red[31])], inc=("tV", 1))
        w = bank_wait(ps_f, [("tV", tg)])
        tmm = op("tensor", "matmul", ps_f[0:1, 0:1], s_ones_col_f32,
                 s_stats[:, 32:33], start=True, stop=True, wait=w,
                 inc=("tT", 1))
        tg = op("vector", "tensor_scalar", s_stats[0:1, 33:34],
                ps_f[0:1, 0:1], 0.5 / (FF * E), 0.5 * EPS_Q, ALU.mult,
                op1=ALU.add, wait=[("tT", tmm)], inc=("tV", 1))
        bank_set(ps_f, "tV", tg)
        w = bank_wait(ps_f, [("tV", tg)])
        tmm = op("tensor", "matmul", ps_f[:, 1:2], s_ones_row_f32,
                 s_stats[0:1, 33:34], start=True, stop=True, wait=w,
                 inc=("tT", 1))
        ts = op("scalar", "copy", s_tau[:, 0:1], ps_f[:, 1:2],
                wait=[("tT", tmm)], inc=("tS", 1))
        bank_set(ps_f, "tS", ts)
        t_tau = op("vector", "tensor_scalar", s_tau[:, 1:2], s_tau[:, 0:1],
                   -1.0, None, ALU.mult, wait=[("tS", ts)], inc=("tV", 1))
        yield
        prev_q = 0
        prev_tp = 0
        prev_st = 0
        stage4 = p4a_stage.rearrange("p (ec q f) -> p ec q f", ec=8, q=2)
        for gp in range(16):
            for i in range(2):
                ft = gp * 2 + i
                qeng, qsem = ("sync", "dS") if i == 0 else ("scalar", "dA")
                w = [("tV", prev_q if gp or i else t_tau)]
                tl = op(qeng, "dma_start", out=p4a_wfsA,
                        in_=src_w1[ft], inc=(qsem, 16), wait=w)
                wv = [(qsem, tl), ("tT", prev_tp)]
                prev_q = quantize_p4a(p4a_pair[:, i * 1024:(i + 1) * 1024],
                                      p4a_wfsA, wv)
                if i == 0:
                    yield
            for ec in range(8):
                bankp = ps_f if ec % 2 == 0 else ps_e
                w = bank_wait(bankp, [("tV", prev_q)] if ec == 0 else ())
                if ec == 0 and prev_st:
                    w.append(("dG", prev_st))
                for i in range(2):
                    op("tensor", "matmul",
                       bankp[:, i * 128:(i + 1) * 128],
                       p4a_pair[:, i * 1024 + ec * 128:
                                i * 1024 + (ec + 1) * 128],
                       s_iden, start=(i == 0), stop=(i == 1),
                       wait=w if i == 0 else (),
                       inc=("tT", 1) if i == 1 else None)
                    w = []
                tv = op("scalar", "copy", p4a_stage[:, ec * 256:(ec + 1) * 256],
                        bankp[:, 0:256],
                        wait=[("tT", at("tT"))], inc=("tS", 1))
                bank_set(bankp, "tS", tv)
                if ec % 2 == 1:
                    yield
            prev_tp = at("tT")
            tev = at("tS")
            for i in range(2):
                ft = gp * 2 + i
                prev_st = op("gpsimd", "dma_start",
                             out=d_q1t[ft].rearrange("e (ec f) -> e ec f",
                                                     ec=8),
                             in_=stage4[:, :, i, :], inc=("dG", 16),
                             wait=[("tS", tev)] if i == 0 else ())
            yield
        while True:
            yield

    def quantize_p4a(dst, srcb, wv):
        wv = list(wv)
        if q_chain[0]:
            wv.append(("tV", q_chain[0]))
        tq = op("vector", "tensor_single_scalar", p4a_B, srcb,
                s_tau[:, 1:2], ALU.is_lt, wait=wv, inc=("tV", 1))
        q_chain[0] = op("vector", "scalar_tensor_tensor", dst, srcb,
                        s_tau[:, 0:1], p4a_B, ALU.is_gt, ALU.subtract,
                        wait=[("tV", tq)], inc=("tV", 1))
        return q_chain[0]

    q_chain = [0]
    p4a_iter = [None]

    def p4a_step():
        if p4a_iter[0] is not None:
            next(p4a_iter[0])

    if phases >= 2:
        t_att = 0
        ebuf_guard = [0, 0]
        if phases >= 4:
            p4a_iter[0] = p4a_gen([("tS", t_p1_done)])
        pairs3 = (ps_c, ps_d, ps_g)
        s_e0 = big8[:, 0:2048]
        s_e1 = big8[:, 2048:4096]
        s_otn2 = big8[:, 4096:4608]
        s_otn2b = big8[:, 4608:5120]
        etv = s_et.rearrange("p (kc q) -> p kc q", kc=16)
        ebuf_of = (s_e0, s_e1)
        exp_done = [0, 0, 0]
        ebuf_guard = [0, 0]
        tpk = [0]
        ST = {}
        pend_group = [None]
        pend_ot = [None]
        grp_rs = []

        def u_scores(band, hh, qb, qg, kh):
            u = 2 * qb + kh
            bank = pairs3[u % 3]
            cq, qhalf = qg // 2, qg % 2
            lhs_q = s_qb[:, hh * 2048 + cq * 256 + qhalf * 128:
                         hh * 2048 + cq * 256 + (qhalf + 1) * 128]
            w = bank_wait(bank)
            if qb == 0 and kh == 0:
                w.append(("tV", rope_v_of[band]))
            for half in range(2):
                rhs = s_kb[:, hh * 2048 + kh * 1024 + half * 512:
                           hh * 2048 + kh * 1024 + (half + 1) * 512]
                op("tensor", "matmul", bank[:, half * 512:(half + 1) * 512],
                   lhs_q, rhs, start=True, stop=True,
                   wait=w if half == 0 else (),
                   inc=("tT", 1) if half == 1 else None)
                w = []
            return at("tT")

        def u_maxes(qb, kh, tmm):
            u = 2 * qb + kh
            c0 = (qb % 3) * 6
            bank = pairs3[u % 3]
            tmax = op("vector", "reduce_max", s_nm[:, c0 + kh:c0 + kh + 1],
                      bank, axis=AX.X, wait=[("tT", tmm)], inc=("tV", 1))
            if kh == 0:
                return op("vector", "tensor_scalar", s_nm[:, c0 + 3:c0 + 4],
                          s_nm[:, c0:c0 + 1], -ISQRT8, None, ALU.mult,
                          wait=[("tV", tmax)], inc=("tV", 1))
            tm = op("vector", "reduce_max", s_nm[:, c0 + 2:c0 + 3],
                    s_nm[:, c0:c0 + 2], axis=AX.X, wait=[("tV", tmax)],
                    inc=("tV", 1))
            tnb = op("vector", "tensor_scalar", s_nm[:, c0 + 4:c0 + 5],
                     s_nm[:, c0 + 2:c0 + 3], -ISQRT8, None, ALU.mult,
                     wait=[("tV", tm)], inc=("tV", 1))
            return op("vector", "scalar_tensor_tensor",
                      s_nm[:, c0 + 5:c0 + 6], s_nm[:, c0:c0 + 1], ISQRT8,
                      s_nm[:, c0 + 4:c0 + 5], ALU.mult, ALU.add,
                      wait=[("tV", tnb)], inc=("tV", 1))

        def u_exps(qb, kh, tnb):
            u = 2 * qb + kh
            bank = pairs3[u % 3]
            c0 = (qb % 3) * 6
            a0 = (qb % 2) * 2
            ebuf = ebuf_of[qb % 2]
            wx = [("tV", tnb)]
            if kh == 0:
                wx.append(("tT", ebuf_guard[qb % 2]))
            tse = op("scalar", "activation", ebuf[:, kh * 1024:(kh + 1) * 1024],
                     bank, AF.Exp,
                     bias=s_nm[:, c0 + 3 + kh:c0 + 4 + kh], scale=ISQRT8,
                     accum_out=s_gs[:, a0 + kh:a0 + kh + 1],
                     wait=wx, inc=("tS", 1))
            bank_set(bank, "tS", tse)
            exp_done[u % 3] = tse
            if kh == 1:
                st = ST[qb]
                st["tf0"] = op("scalar", "activation",
                               s_gs[:, 24 + qb % 2:25 + qb % 2],
                               s_nm[:, (qb % 3) * 6 + 5:(qb % 3) * 6 + 6],
                               AF.Exp, inc=("tS", 1))
                st["f0g"] = op("scalar", "copy", s_tau[0:1, 2:3],
                               s_nm[0:1, (qb % 3) * 6 + 4:(qb % 3) * 6 + 5],
                               inc=("tS", 1))
            return tse

        def u_sigma_d0(qb):
            st = ST[qb]
            a0 = (qb % 2) * 2
            tsg = op("vector", "scalar_tensor_tensor",
                     s_gs[:, 12 + qb % 4:13 + qb % 4], s_gs[:, a0:a0 + 1],
                     s_gs[:, 24 + qb % 2:25 + qb % 2],
                     s_gs[:, a0 + 1:a0 + 2], ALU.mult, ALU.add,
                     wait=[("tS", st["f0g"])], inc=("tV", 1))
            st["trs"] = op("vector", "reciprocal",
                           s_gs[:, 16 + qb % 8:17 + qb % 8],
                           s_gs[:, 12 + qb % 4:13 + qb % 4],
                           wait=[("tV", tsg)], inc=("tV", 1))
            st["td0"] = op("scalar", "activation",
                           s_dmat[:, (qb % 2) * 128:(qb % 2) * 128 + 128],
                           s_iden, AF.Identity,
                           scale=s_gs[:, 24 + qb % 2:25 + qb % 2],
                           wait=[("tS", st["tf0"]),
                                 ("tT", ebuf_guard[qb % 2])],
                           inc=("tS", 1))

        def u_transposes(qb, kh):
            st = ST[qb]
            qj = st["qj"]
            ebuf = ebuf_of[qb % 2]
            d0 = s_dmat[:, (qb % 2) * 128:(qb % 2) * 128 + 128]
            for kg in range(2):
                bank = ps_a if tpk[0] % 2 == 0 else ps_b
                tpk[0] += 1
                w = bank_wait(bank)
                if kg == 0 and kh == 0:
                    w += [("tS", st["td0"]), ("tS", exp_done[(2 * qb + 1) % 3])]
                for t4 in range(4):
                    kc = kh * 8 + kg * 4 + t4
                    tl4 = op("tensor", "matmul",
                             bank[:, t4 * 128:(t4 + 1) * 128],
                             ebuf[:, kc * 128:(kc + 1) * 128],
                             d0 if kc < 8 else s_iden,
                             start=(t4 == 0), stop=(t4 == 3),
                             wait=w if t4 == 0 else (),
                             inc=("tT", 1) if t4 == 3 else None)
                    w = []
                kg0 = kh * 2 + kg
                dstv = etv[:, kg0 * 4:kg0 * 4 + 4, qj * 128:(qj + 1) * 128]
                srcv = bank.rearrange("p (a b) -> p a b", a=4)
                if kg0 != 3:
                    tv = op("scalar", "copy", dstv, srcv,
                            wait=[("tT", at("tT"))], inc=("tS", 1))
                    bank_set(bank, "tS", tv)
                else:
                    tv = op("vector", "tensor_copy", dstv, srcv,
                            wait=[("tT", at("tT"))], inc=("tV", 1))
                    bank_set(bank, "tV", tv)
            if kh == 1:
                ebuf_guard[qb % 2] = tl4
                st["t_et"] = [("tS", at("tS")), ("tV", at("tV"))]

        def emit_group_av(gst):
            hh = gst["hh"]
            bankA = ps_a if tpk[0] % 2 == 0 else ps_b
            bankB = ps_b if tpk[0] % 2 == 0 else ps_a
            tpk[0] += 2
            ws = [bank_wait(bankA, list(gst["t_et"])), bank_wait(bankB)]
            tmm2 = [0, 0]
            for kk in range(8):
                for j in range(2):
                    kc = j * 8 + kk
                    bank = bankA if j == 0 else bankB
                    lhs = s_vb[:, (hh * 2 + kc % 2) * 1024 + (kc // 2) * 128:
                               (hh * 2 + kc % 2) * 1024 + (kc // 2 + 1) * 128]
                    tmm2[j] = op("tensor", "matmul", bank, lhs,
                                 s_et[:, kc * 512:(kc + 1) * 512],
                                 start=(kk == 0), stop=(kk == 7),
                                 wait=ws[j] if kk == 0 else (),
                                 inc=("tT", 1) if kk == 7 else None)
                    ws[j] = []
            tA = op("vector", "tensor_copy", s_otn2, bankA,
                    wait=[("tT", tmm2[0])], inc=("tV", 1))
            tB = op("vector", "tensor_copy", s_otn2b, bankB,
                    wait=[("tT", tmm2[1])], inc=("tV", 1))
            bank_set(bankA, "tV", tA)
            bank_set(bankB, "tV", tB)
            gst["t_oev"] = [("tV", tB)]

        def emit_group(gst):
            h, g = gst["h"], gst["g"]
            bank2 = ps_a if tpk[0] % 2 == 0 else ps_b
            tpk[0] += 1
            w = bank_wait(bank2, list(gst["t_oev"]))
            for qt in range(4):
                for srcb in (s_otn2, s_otn2b):
                    op("tensor", "matmul",
                       bank2[:, qt * 128:(qt + 1) * 128],
                       srcb[:, qt * 128:(qt + 1) * 128], s_iden,
                       start=(qt == 0 and srcb is s_otn2),
                       stop=(qt == 3 and srcb is s_otn2b),
                       wait=w if (qt == 0 and srcb is s_otn2) else (),
                       inc=("tT", 1) if (qt == 3 and srcb is s_otn2b)
                       else None)
                    w = []
            qid0 = gst["qb"] - 3
            for qt in range(4):
                tile = g * 4 + qt
                wq = [("tT", at("tT"))] if qt == 0 else []
                wq.append(("tV", gst["trs"][qt]))
                op("vector", "tensor_scalar",
                   s_attn[:, tile * 1024 + h * 128:tile * 1024 + (h + 1) * 128],
                   bank2[:, qt * 128:(qt + 1) * 128],
                   s_gs[:, 16 + (qid0 + qt) % 8:17 + (qid0 + qt) % 8],
                   None, ALU.mult,
                   wait=wq, inc=("tV", 1) if qt == 3 else None)
            bank_set(bank2, "tV", at("tV"))

        def drive_band(band, h0):
            units = []
            for hh in range(2):
                h = h0 + hh
                for g in range(4):
                    for qj in range(4):
                        qg = g * 4 + qj
                        qb = hh * 16 + qg
                        for kh in range(2):
                            units.append((hh, h, g, qj, qb, qg, kh))
            n = len(units)
            for ui in range(n + 5):
                if 0 <= ui - 2 < n:
                    qb2, kh2 = units[ui - 2][4], units[ui - 2][6]
                    u_exps(qb2, kh2, ST[qb2]["tnb%d" % kh2])
                if ui % 2 == 0:
                    if pend_ot[0] is not None:
                        emit_group(pend_ot[0])
                        pend_ot[0] = None
                    elif pend_group[0] is not None:
                        emit_group_av(pend_group[0])
                        pend_ot[0] = pend_group[0]
                        pend_group[0] = None
                if 0 <= ui - 4 < n:
                    qb4, kh4 = units[ui - 4][4], units[ui - 4][6]
                    if kh4 == 0:
                        u_sigma_d0(qb4)
                    u_transposes(qb4, kh4)
                    if kh4 == 1:
                        stp = ST[qb4]
                        grp_rs.append(stp["trs"])
                        if stp["qj"] == 3:
                            assert pend_group[0] is None
                            pend_group[0] = dict(stp, trs=grp_rs[-4:])
                        del ST[qb4]
                if ui < n:
                    hh, h, g, qj, qb, qg, kh = units[ui]
                    ST.setdefault(qb, {"qb": qb, "hh": hh, "h": h, "g": g,
                                       "qj": qj})
                    tmm = u_scores(band, hh, qb, qg, kh)
                    ST[qb]["tnb%d" % kh] = u_maxes(qb, kh, tmm)
                if ui % 2 == 1:
                    p4a_step()
            if pend_group[0] is not None:
                emit_group_av(pend_group[0])
                pend_ot[0] = pend_group[0]
                pend_group[0] = None
            if pend_ot[0] is not None:
                emit_group(pend_ot[0])
                pend_ot[0] = None

        rope_v_of = {}
        for band in range(4):
            barrier()
            h0 = 2 * band
            bcol = band * 512
            ev_qk = {id(s_qb): [0] * 8, id(s_kb): [0] * 8}

            def emit_vt(vt):
                bank = ps_c if vt % 2 == 0 else ps_d
                w = bank_wait(bank)
                for ec in range(8):
                    op("tensor", "matmul", bank[:, 0:512],
                       s_at[:, ec * 2048 + bcol + vt * 128:
                            ec * 2048 + bcol + (vt + 1) * 128],
                       s_wt3[:, ec * 1024: ec * 1024 + 512],
                       start=(ec == 0), stop=(ec == 7),
                       wait=w if ec == 0 else ())
                    w = []
                    op("tensor", "matmul", bank[:, 512:1024],
                       s_at[:, ec * 2048 + bcol + vt * 128:
                            ec * 2048 + bcol + (vt + 1) * 128],
                       s_wt3[:, ec * 1024 + 512: (ec + 1) * 1024],
                       start=(ec == 0), stop=(ec == 7),
                       inc=("tT", 1) if ec == 7 else None)
                tv = op("scalar", "copy", s_vb[:, vt * 1024:(vt + 1) * 1024],
                        bank, wait=[("tT", at("tT"))], inc=("tS", 1))
                bank_set(bank, "tS", tv)

            rope_pend = []

            def emit_rope(dstb, cs):
                d3r = dstb.rearrange("p (h c2 x) -> p h c2 x", h=2, c2=8)
                for c in cs:
                    bankr = (ps_g[:, 0:512] if c % 2 == 0
                             else ps_g[:, 512:1024])
                    w = bank_wait(ps_g, [("tS", ev_qk[id(dstb)][c])])
                    tmm = op("tensor", "matmul", bankr, s_pt, d3r[:, :, c, :],
                             start=True, stop=True, wait=w, inc=("tT", 1))
                    tabc = s_cos[:, c * 256:(c + 1) * 256]
                    tabs = s_sin[:, c * 256:(c + 1) * 256]
                    for hh in range(2):
                        ss = dstb[:, hh * 2048 + c * 256:
                                  hh * 2048 + (c + 1) * 256]
                        rr = bankr[:, hh * 256:(hh + 1) * 256]
                        op("vector", "tensor_tensor", ss, ss, tabc, ALU.mult,
                           wait=[("tT", tmm)] if hh == 0 else (),
                           inc=("tV", 1))
                        tv2 = op("vector", "tensor_tensor", rr, rr, tabs,
                                 ALU.mult, inc=("tV", 1))
                        op("vector", "tensor_tensor", ss, ss, rr, ALU.add,
                           wait=[("tV", tv2)], inc=("tV", 1))
                    bank_set(ps_g, "tV", at("tV"))

            qkp = 0
            for (wt, dstb) in ((s_wt1, s_qb), (s_wt2, s_kb)):
                for cp in range(4):
                    cs = (2 * cp, 2 * cp + 1)
                    ws = [bank_wait(ps_a), bank_wait(ps_b)]
                    tmm2 = [0, 0]
                    for ec in range(8):
                        for j in range(2):
                            c = cs[j]
                            bank = ps_a if j == 0 else ps_b
                            tmm2[j] = op(
                                "tensor", "matmul", bank,
                                wt[:, ec * 1024 + c * 128:
                                   ec * 1024 + (c + 1) * 128],
                                s_at[:, ec * 2048 + bcol:
                                     ec * 2048 + bcol + 512],
                                start=(ec == 0), stop=(ec == 7),
                                wait=ws[j] if ec == 0 else (),
                                inc=("tT", 1) if ec == 7 else None)
                            ws[j] = []
                    for j in range(2):
                        c = cs[j]
                        bank = ps_a if j == 0 else ps_b
                        d3 = dstb.rearrange("p (h c2 x) -> p h c2 x",
                                            h=2, c2=8)
                        tv = op("scalar", "copy", d3[:, :, c, :],
                                bank.rearrange("p (h x) -> p h x", h=2),
                                wait=[("tT", tmm2[j])], inc=("tS", 1))
                        ev_qk[id(dstb)][c] = tv
                        bank_set(bank, "tS", tv)
                    rope_pend.append((dstb, cs))
                    if len(rope_pend) > 1:
                        emit_rope(*rope_pend.pop(0))
                    if qkp % 2 == 1:
                        emit_vt(qkp // 2)
                    qkp += 1
            while rope_pend:
                emit_rope(*rope_pend.pop(0))
            t_b1_s = at("tS")
            if "vband" in tap and band == 0:
                op("gpsimd", "dma_start", out=tap["vband"], in_=s_vb,
                   wait=[("tS", t_b1_s)], inc=("dG", 16))
            rope_v = at("tV")
            if "qband" in tap and band == 0:
                op("gpsimd", "dma_start", out=tap["qband"], in_=s_qb,
                   wait=[("tV", rope_v)], inc=("dG", 16))
                op("gpsimd", "dma_start", out=tap["kband"], in_=s_kb,
                   wait=[("tV", rope_v)], inc=("dG", 16))
            rope_v_of[band] = rope_v
            drive_band(band, h0)
        if "attn" in tap:
            op("gpsimd", "dma_start", out=tap["attn"], in_=s_attn,
               wait=[("tS", at("tS"))], inc=("dG", 16))

    def gamma_pass_gen(d_w, rows, cols, gcol, out):
        npieces = (rows // 128) * (cols // 1024)
        per_row = cols // 1024
        assert npieces <= 32
        src3 = d_w.rearrange("(t p) (q e) -> t q p e", p=128, e=1024)
        red = [0] * npieces
        gbufs = (s_wfs4b[:, 0:1024], s_wfs4b[:, 1024:2048],
                 s_wfs.bitcast(F32)[:, 0:1024], s_wfs.bitcast(F32)[:, 1024:2048])
        gsems = ("dS", "dA", "dB", "dC")
        for t in range(npieces):
            buf = gbufs[t % 4]
            qeng = ("sync", "scalar")[t % 2]
            qsem = gsems[t % 4]
            w = [("tV", red[t - 4])] if t >= 4 else ()
            tl = op(qeng, "dma_start", out=buf,
                    in_=src3[t // per_row, t % per_row], inc=(qsem, 16),
                    wait=w)
            red[t] = op("vector", "reduce_sum", s_gs[:, t:t + 1], buf,
                        axis=AX.X, apply_absolute_value=True,
                        wait=[(qsem, tl)], inc=("tV", 1))
            yield
        tg = op("vector", "reduce_sum", s_nm[:, 22:23], s_gs[:, 0:npieces],
                axis=AX.X, wait=[("tV", red[npieces - 1])], inc=("tV", 1))
        w = bank_wait(ps_f, [("tV", tg)])
        tmm = op("tensor", "matmul", ps_f[0:1, 0:1], s_ones_col_f32,
                 s_nm[:, 22:23], start=True, stop=True, wait=w, inc=("tT", 1))
        tg = op("vector", "tensor_scalar", s_nm[0:1, 20:21], ps_f[0:1, 0:1],
                0.5 / (rows * cols), 0.5 * EPS_Q, ALU.mult, op1=ALU.add,
                wait=[("tT", tmm)], inc=("tV", 1))
        bank_set(ps_f, "tV", tg)
        w = bank_wait(ps_f, [("tV", tg)])
        tmm = op("tensor", "matmul", ps_f[:, 1:2], s_ones_row_f32,
                 s_nm[0:1, 20:21], start=True, stop=True, wait=w,
                 inc=("tT", 1))
        ts = op("scalar", "copy", s_tau[:, gcol:gcol + 1], ps_f[:, 1:2],
                wait=[("tT", tmm)], inc=("tS", 1))
        bank_set(ps_f, "tS", ts)
        out[0] = op("vector", "tensor_scalar", s_tau[:, gcol + 1:gcol + 2],
                    s_tau[:, gcol:gcol + 1], -1.0, None, ALU.mult,
                    wait=[("tS", ts)], inc=("tV", 1))
        yield

    q_chain = [0]

    def quantize_into(dst, src, gcol, wv):
        wv = list(wv)
        if q_chain[0]:
            wv.append(("tV", q_chain[0]))
        tq = op("vector", "tensor_single_scalar", s_abq[:, 0:src.shape[1]],
                src, s_tau[:, gcol + 1:gcol + 2], ALU.is_lt, wait=wv,
                inc=("tV", 1))
        q_chain[0] = op("vector", "scalar_tensor_tensor", dst, src,
                        s_tau[:, gcol:gcol + 1], s_abq[:, 0:src.shape[1]],
                        ALU.is_gt, ALU.subtract, wait=[("tV", tq)],
                        inc=("tV", 1))
        return q_chain[0]

    def p4b_gen():
        tau_out = [0]
        for _ in gamma_pass_gen(dw["wf2"], E, FF, 2, tau_out):
            yield
        t_tau = tau_out[0]
        prev_q = [0, 0, 0, 0]
        qbufs = (s_wfs4b[:, 0:1024], s_wfs4b[:, 1024:2048],
                 s_wfs.bitcast(F32)[:, 0:1024],
                 s_wfs.bitcast(F32)[:, 1024:2048])
        qsems4 = ("dS", "dA", "dB", "dC")
        tp_hist = [0, 0]
        for pair in range(4):
            q2sb = s_q2s2[pair % 2]
            for i in range(2):
                et = pair * 2 + i
                for qq in range(4):
                    n4 = qq
                    buf = qbufs[n4]
                    qeng = ("sync", "scalar")[qq % 2]
                    qsem = qsems4[n4]
                    w = []
                    if pair > 0 or i > 0:
                        w.append(("tV", prev_q[n4]))
                    else:
                        w.append(("tV", t_tau))
                    tl = op(qeng, "dma_start", out=buf,
                            in_=dw["wf2"][et * 128:(et + 1) * 128,
                                          qq * 1024:(qq + 1) * 1024],
                            inc=(qsem, 16), wait=w)
                    wv = [(qsem, tl)]
                    if pair == 0 and i == 0 and qq == 0:
                        wv.append(("tV", t_tau))
                    if i == 0 and qq < 2 and tp_hist[pair % 2]:
                        wv.append(("tT", tp_hist[pair % 2]))
                    prev_q[n4] = quantize_into(
                        q2sb[:, i * 4096 + qq * 1024:
                             i * 4096 + (qq + 1) * 1024], buf, 2, wv)
                    yield
            tq = at("tV")
            for fcg in range(4):
                for fc in range(fcg * 8, fcg * 8 + 8):
                    bank = ps_a if fc % 2 == 0 else ps_b
                    w = bank_wait(bank, [("tV", tq)])
                    for i in range(2):
                        op("tensor", "matmul",
                           bank[:, i * 128:(i + 1) * 128],
                           q2sb[:, i * 4096 + fc * 128:
                                i * 4096 + (fc + 1) * 128],
                           s_iden, start=True, stop=True,
                           wait=w if i == 0 else (),
                           inc=("tT", 1) if i == 1 else None)
                        w = []
                    tv = op("scalar", "copy",
                            s_q2t[:, fc * 1024 + pair * 256:
                                  fc * 1024 + (pair + 1) * 256],
                            bank[:, 0:256], wait=[("tT", at("tT"))],
                            inc=("tS", 1))
                    bank_set(bank, "tS", tv)
                yield
            tp_hist[pair % 2] = at("tT")
        while True:
            yield

    p4b_iter = [None]

    def p4b_step():
        if p4b_iter[0] is not None:
            next(p4b_iter[0])

    if phases >= 3:
        barrier()
        prev_stats_tV = 0
        prev_add2 = [0, 0]
        prev_st2 = [0, 0]
        prev_ts3 = 0
        t_embk = [0] * 16
        for k in range(16):
            u, rb = k // 8, k % 8
            bank = ps_c if k % 2 == 0 else ps_d
            w = bank_wait(bank)
            for c in range(8):
                lhs = s_wtop[:, c * 1024 + rb * 128: c * 1024 + (rb + 1) * 128]
                rhs = s_attn[:, (2 * c + u) * 1024:(2 * c + u + 1) * 1024]
                op("tensor", "matmul", bank[:, 0:512], lhs, rhs[:, 0:512],
                   start=(c == 0), stop=(c == 7), wait=w if c == 0 else ())
                w = []
                op("tensor", "matmul", bank[:, 512:1024], lhs,
                   rhs[:, 512:1024], start=(c == 0), stop=(c == 7),
                   inc=("tT", 1) if c == 7 else None)
            wy = [("tT", at("tT"))]
            if prev_stats_tV:
                wy.append(("tV", prev_stats_tV))
            if prev_ts3:
                wy.append(("tS", prev_ts3))
            tev = op("scalar", "copy", s_y3, bank, wait=wy, inc=("tS", 1))
            bank_set(bank, "tS", tev)
            row = k * 128
            eng3, sem3 = ("sync", "dB") if k % 2 == 0 else ("scalar", "dC")
            we = [("tV", prev_add2[k % 2])] if prev_add2[k % 2] else []
            t_embk[k] = op(eng3, "dma_start", out=s_emb3d[k % 2],
                           in_=d_emb[row:row + 128, :], wait=we,
                           inc=(sem3, 16))
            op("vector", "bn_stats", s_stats[:, 0:6], s_y3[:, 0:512],
               wait=[("tS", tev)])
            prev_stats_tV = op("vector", "bn_stats", s_stats[:, 6:12],
                               s_y3[:, 512:1024], inc=("tV", 1))
            tg = op("vector", "bn_aggr", s_bn4[:, 0:2], s_stats[:, 0:12],
                    wait=[("tV", prev_stats_tV)], inc=("tV", 1))
            tg = op("vector", "tensor_scalar", s_rstd4[:, 0:1], s_bn4[:, 1:2],
                    EPS_LN, None, ALU.add, wait=[("tV", tg)], inc=("tV", 1))
            ts1 = op("scalar", "activation", s_rstd4[:, 0:1], s_rstd4[:, 0:1],
                     AF.Ln, wait=[("tV", tg)], inc=("tS", 1))
            ts2 = op("scalar", "activation", s_rstd4[:, 0:1], s_rstd4[:, 0:1],
                     AF.Exp, scale=-0.5, wait=[("tS", ts1)], inc=("tS", 1))
            tg = op("vector", "tensor_tensor", s_mean4[:, 0:1], s_bn4[:, 0:1],
                    s_rstd4[:, 0:1], ALU.mult, wait=[("tS", ts2)],
                    inc=("tV", 1))
            tg = op("vector", "tensor_scalar", s_mean4[:, 0:1],
                    s_mean4[:, 0:1], -1.0, None, ALU.mult,
                    wait=[("tV", tg)], inc=("tV", 1))
            ts3 = op("scalar", "activation", s_n3, s_y3, AF.Identity,
                     bias=s_mean4[:, 0:1], scale=s_rstd4[:, 0:1],
                     wait=[("tV", tg)], inc=("tS", 1))
            prev_ts3 = ts3
            wadd = [("tS", ts3), (sem3, t_embk[k])]
            if prev_st2[k % 2]:
                wadd.append((("dG", "dV")[k % 2], prev_st2[k % 2]))
            prev_add2[k % 2] = op("vector", "tensor_tensor", s_x1t3d[k % 2],
                                  s_emb3d[k % 2], s_n3, ALU.add, wait=wadd,
                                  inc=("tV", 1))
            prev_st2[k % 2] = op("gpsimd", "dma_start",
                                 out=d_x1[row:row + 128, :],
                                 in_=s_x1t3d[k % 2],
                                 wait=[("tV", prev_add2[k % 2])],
                                 inc=(("dG", "dV")[k % 2], 16))
            for _ in range(3):
                p4b_step()
        if "x1" in tap:
            op("gpsimd", "dma_start", out=tap["x1"], in_=d_x1,
               wait=[("dG", at("dG"))], inc=("dG", 16))

    if phases >= 4:
        for _ in range(120):
            p4a_step()
        if "q1tt" in tap:
            op("gpsimd", "dma_start", out=tap["q1tt"], in_=d_q1t,
               wait=[("dG", at("dG"))], inc=("dG", 16))
        if "tau" in tap:
            op("gpsimd", "dma_start", out=tap["tau"][:, 0:2], in_=s_tau[:, 0:2],
               wait=[("tV", at("tV"))], inc=("dG", 16))

        barrier()
        p4b_iter[0] = p4b_gen()
        for _ in range(130):
            p4b_step()
        if "q2t" in tap:
            op("gpsimd", "dma_start", out=tap["q2t"], in_=s_q2t,
               wait=[("tS", at("tS"))], inc=("dG", 16))

    if phases >= 5:
        barrier()
        t_b2l = op("sync", "dma_start", out=s_b2row, in_=dw["b2"][None, :],
                   inc=("dS", 16))
        w = bank_wait(ps_c, [("dS", t_b2l)])
        for half in range(2):
            op("tensor", "matmul", ps_c[:, half * 512:(half + 1) * 512],
               s_ones_row_f32, s_b2row[:, half * 512:(half + 1) * 512],
               start=True, stop=True, wait=w if half == 0 else (),
               inc=("tT", 1) if half == 1 else None)
            w = []
        t_b2 = op("scalar", "copy", s_b2b, ps_c, wait=[("tT", at("tT"))],
                  inc=("tS", 1))
        bank_set(ps_c, "tS", t_b2)

        prev_hev = [0] * 32
        prev_out_guard = [0, 0]
        prev_dm5 = [0]
        prev_cast5 = [0]
        tx1_hist = [None, None]
        prev_x1sem = [0, 0]
        q1t_tq = [0] * 32
        q1rez_tq = [0] * 32
        q1sems = ("dS", "dA", "dB", "dC")
        tout_hist = [[0] * 4, [0] * 4]
        t_ffn2mm = [0, 0]
        def emit_x1_in(ch):
            if ch >= 4:
                return
            x1b = s_x1q2[ch % 2]
            tx1 = []
            for i in range(4):
                row = (ch * 4 + i) * 128
                w = []
                if tout_hist[ch % 2][i]:
                    w.append(("tV", tout_hist[ch % 2][i]))
                if prev_x1sem[i % 2]:
                    w.append((("dX", "dY")[i % 2], prev_x1sem[i % 2]))
                prev_x1sem[i % 2] = op(("sync", "scalar")[i % 2], "dma_start",
                                       out=x1b[:, i * 1024:(i + 1) * 1024],
                                       in_=d_x1[row:row + 128, :],
                                       inc=(("dX", "dY")[i % 2], 16), wait=w)
                tx1.append(prev_x1sem[i % 2])
            tx1_hist[ch % 2] = tx1

        barrier()
        emit_x1_in(0)
        for ch in range(4):
            x1b = s_x1q2[ch % 2]
            a2b = s_a2t5x[ch % 2]
            tx1 = tx1_hist[ch % 2]
            prev_dm_c = prev_dm5[0]
            prev_cast_c = prev_cast5[0]
            for g2 in range(2):
                x0 = x1b[:, (g2 * 2) * 1024:(g2 * 2 + 1) * 1024]
                x1 = x1b[:, (g2 * 2 + 1) * 1024:(g2 * 2 + 2) * 1024]
                op("vector", None, wait=[("dX", tx1[2]), ("dY", tx1[3])])
                t_rstd = ln_stats_pair(x0, x1, prev_cast_c)
                prev_cast_c, prev_dm_c = ln_transpose_pair(
                    x0, x1, s_xc5, g2 * 256, t_rstd, a2b, 512, prev_dm_c)
            prev_dm5[0] = prev_dm_c
            prev_cast5[0] = prev_cast_c
            t_a2 = at("tS")
            if "a2t" in tap and ch == 0:
                op("gpsimd", "dma_start", out=tap["a2t"], in_=a2b,
                   wait=[("tS", t_a2)], inc=("dG", 16))
            for fp in range(16):
                fbs = (2 * fp, 2 * fp + 1)
                tqs = []
                for fb in fbs:
                    qbuf = s_q1rez[:, fb * 1024:(fb + 1) * 1024]
                    qsem = q1sems[fb % 4]
                    if ch > 0:
                        tqs.append((qsem, q1rez_tq[fb]))
                        continue
                    qeng = ("sync", "scalar")[fb % 2]
                    w = [("tS", prev_hev[fb - 4])] if fb >= 4 else []
                    tq = op(qeng, "dma_start", out=qbuf, in_=d_q1t[fb],
                            inc=(qsem, 16), wait=w)
                    q1rez_tq[fb] = tq
                    tqs.append((qsem, tq))
                ws = [bank_wait(ps_a, [tqs[0], ("tS", t_a2)]),
                      bank_wait(ps_b, [tqs[1], ("tS", t_a2)])]
                tmm2 = [0, 0]
                for ec in range(8):
                    for j in range(2):
                        fb = fbs[j]
                        bank = ps_a if j == 0 else ps_b
                        qb5 = s_q1rez[:, fb * 1024:(fb + 1) * 1024]
                        tmm2[j] = op("tensor", "matmul", bank,
                                     qb5[:, ec * 128:(ec + 1) * 128],
                                     a2b[:, ec * 512:(ec + 1) * 512],
                                     start=(ec == 0), stop=(ec == 7),
                                     wait=ws[j] if ec == 0 else (),
                                     inc=("tT", 1) if ec == 7 else None)
                        ws[j] = []
                for j in range(2):
                    fb = fbs[j]
                    bank = ps_a if j == 0 else ps_b
                    wev = [("tT", tmm2[j])]
                    if fb == 0 and t_ffn2mm[0]:
                        wev.append(("tT", t_ffn2mm[0]))
                    prev_hev[fb] = op("scalar", "activation",
                                      s_ht[:, fb * 512:(fb + 1) * 512], bank,
                                      AF.Relu, bias=s_b1t[:, fb:fb + 1],
                                      wait=wev, inc=("tS", 1))
                    bank_set(bank, "tS", prev_hev[fb])
            emit_x1_in(ch + 1)
            t_h = at("tS")
            if "ht" in tap and ch == 0:
                op("gpsimd", "dma_start", out=tap["ht"], in_=s_ht,
                   wait=[("tS", t_h)], inc=("dG", 16))
            for sb in range(4):
                bank = ps_c if sb % 2 == 0 else ps_d
                fbuf = s_f5 if sb % 2 == 0 else s_f5b
                obuf = s_o5 if sb % 2 == 0 else s_o5b
                w = bank_wait(bank, [("tS", t_h)])
                for fc in range(32):
                    lhs = s_ht[:, fc * 512 + sb * 128:
                               fc * 512 + (sb + 1) * 128]
                    op("tensor", "matmul", bank[:, 0:512], lhs,
                       s_q2t[:, fc * 1024: fc * 1024 + 512],
                       start=(fc == 0), stop=(fc == 31),
                       wait=w if fc == 0 else ())
                    w = []
                    op("tensor", "matmul", bank[:, 512:1024], lhs,
                       s_q2t[:, fc * 1024 + 512:(fc + 1) * 1024],
                       start=(fc == 0), stop=(fc == 31),
                       inc=("tT", 1) if fc == 31 else None)
                wf = [("tT", at("tT")), ("tS", t_b2)]
                if prev_out_guard[sb % 2]:
                    wf.append(("tV", prev_out_guard[sb % 2]))
                tfv = op("vector", "tensor_tensor", fbuf, bank, s_b2b,
                         ALU.add, wait=wf, inc=("tV", 1))
                bank_set(bank, "tV", tfv)
                op("vector", "bn_stats", s_stats[:, 24:30], fbuf[:, 0:512],
                   wait=[("tV", tfv)])
                tg = op("vector", "bn_stats", s_stats[:, 30:36],
                        fbuf[:, 512:1024], inc=("tV", 1))
                tg = op("vector", "bn_aggr", s_bn4[:, 4:6], s_stats[:, 24:36],
                        wait=[("tV", tg)], inc=("tV", 1))
                tg = op("vector", "tensor_scalar", s_rstd4[:, 2:3],
                        s_bn4[:, 5:6], EPS_LN, None, ALU.add,
                        wait=[("tV", tg)], inc=("tV", 1))
                ts1 = op("scalar", "activation", s_rstd4[:, 2:3],
                         s_rstd4[:, 2:3], AF.Ln, wait=[("tV", tg)],
                         inc=("tS", 1))
                ts2 = op("scalar", "activation", s_rstd4[:, 2:3],
                         s_rstd4[:, 2:3], AF.Exp, scale=-0.5,
                         wait=[("tS", ts1)], inc=("tS", 1))
                tg = op("vector", "tensor_tensor", s_mean4[:, 2:3],
                        s_bn4[:, 4:5], s_rstd4[:, 2:3], ALU.mult,
                        wait=[("tS", ts2)], inc=("tV", 1))
                tg = op("vector", "tensor_scalar", s_mean4[:, 2:3],
                        s_mean4[:, 2:3], -1.0, None, ALU.mult,
                        wait=[("tV", tg)], inc=("tV", 1))
                ts3 = op("scalar", "activation", obuf, fbuf, AF.Identity,
                         bias=s_mean4[:, 2:3], scale=s_rstd4[:, 2:3],
                         wait=[("tV", tg)], inc=("tS", 1))
                tout = op("vector", "tensor_tensor", obuf,
                          x1b[:, sb * 1024:(sb + 1) * 1024], obuf, ALU.add,
                          wait=[("tS", ts3)], inc=("tV", 1))
                tout_hist[ch % 2][sb] = tout
                row = (ch * 4 + sb) * 128
                t_st = op("gpsimd", "dma_start", out=d_out[row:row + 128, :],
                          in_=obuf, wait=[("tV", tout)], inc=("dG", 16))
                prev_out_guard[sb % 2] = op("vector", "engine_nop",
                                            wait=[("dG", t_st)],
                                            inc=("tV", 1))
            t_ffn2mm[0] = at("tT")
    else:
        op("gpsimd", "dma_start", out=d_out[0:128, :],
           in_=big1.bitcast(F32)[:, 0:1024],
           wait=[("tS", at("tS")), ("tV", at("tV"))], inc=("dG", 16))

    barrier(("gpsimd",))
    pr.emit()
    return nc, pr.nops


def kernel(L0_attn_emb, L0_attn_W1, L0_attn_W2, L0_attn_W3, L0_attn_Wo,
           L0_ffn_W1, L0_ffn_b1, L0_ffn_W2, L0_ffn_b2):
    C = host_constants()
    nc, _ = build_nc(taps=TAPS, phases=PHASES)
    emb = np.ascontiguousarray(np.asarray(L0_attn_emb, dtype=np.float32))
    base = {
        "w1": np.ascontiguousarray(L0_attn_W1, dtype=np.float32),
        "w2": np.ascontiguousarray(L0_attn_W2, dtype=np.float32),
        "w3": np.ascontiguousarray(L0_attn_W3, dtype=np.float32),
        "wo": np.ascontiguousarray(L0_attn_Wo, dtype=np.float32),
        "wf1": np.ascontiguousarray(L0_ffn_W1, dtype=np.float32),
        "b1": np.ascontiguousarray(L0_ffn_b1, dtype=np.float32),
        "wf2": np.ascontiguousarray(L0_ffn_W2, dtype=np.float32),
        "b2": np.ascontiguousarray(L0_ffn_b2, dtype=np.float32),
    }
    base.update(C)
    in_maps = [dict(base, emb=emb[b]) for b in range(8)]
    res = run_bass_kernel_spmd(nc, in_maps, core_ids=list(range(8)))
    return np.stack([np.asarray(r["out"]) for r in res.results])
